# version 15
# baseline (speedup 1.0000x reference)
import math
import numpy as np
import ml_dtypes
from contextlib import ExitStack

import concourse.bass as bass
import concourse.mybir as mybir
from concourse.bass_utils import run_bass_kernel_spmd

F32 = mybir.dt.float32
BF16 = mybir.dt.bfloat16
AF = mybir.ActivationFunctionType
ALU = mybir.AluOpType
AX = mybir.AxisListType

NC = 8
TPC = 1024
NT = 8
D = 4096
NKV = 8
NQH = 32
HD = 128
NG = 32
EPS = 1e-5
SHIFT = 14.0
ISQ = 1.0 / math.sqrt(HD)

ENGS = ("sync", "tensor", "scalar", "vector", "gpsimd")


class Plan:
    def __init__(self):
        self.q = {e: [] for e in ENGS}
        self.counts = {}

    def sem_init(self, names):
        for n in names:
            self.counts[n] = 0

    def inc(self, name, n=1):
        self.counts[name] += n
        return self.counts[name]

    def val(self, name):
        return self.counts[name]

    def step(self, eng, builder):
        self.q[eng].append(builder)


def host_prepare(inputs):
    bf = ml_dtypes.bfloat16
    x = np.asarray(inputs["L0_attn_hidden_states"])

    def wq_t(qw):
        return np.ascontiguousarray(np.asarray(qw).T.astype(np.float32)).astype(bf)

    def sc_t(s):
        s = np.asarray(s, dtype=np.float32)
        n = s.shape[1]
        out = s.T.reshape(n // 128, 128, NG).transpose(1, 0, 2).reshape(128, (n // 128) * NG)
        return np.ascontiguousarray(out)

    shared = {
        "qwT_q": wq_t(inputs["L0_attn_qw_q"]),
        "qwT_k": wq_t(inputs["L0_attn_qw_k"]),
        "qwT_v": wq_t(inputs["L0_attn_qw_v"]),
        "qwT_o": wq_t(inputs["L0_attn_qw_o"]),
        "sT_q": sc_t(inputs["L0_attn_scales_q"]),
        "sT_k": sc_t(inputs["L0_attn_scales_k"]),
        "sT_v": sc_t(inputs["L0_attn_scales_v"]),
        "sT_o": sc_t(inputs["L0_attn_scales_o"]),
        "bwT": np.ascontiguousarray(np.asarray(inputs["L0_ffn_base_w"], dtype=np.float32).T),
        "bvec": np.ascontiguousarray(
            np.asarray(inputs["L0_ffn_base_b"], dtype=np.float32).reshape(32, 128).T),
        "acatT": np.ascontiguousarray(
            np.asarray(inputs["L0_ffn_lora_A"], dtype=np.float32).transpose(2, 0, 1).reshape(D, 128)),
        "bcatT": np.ascontiguousarray(
            np.asarray(inputs["L0_ffn_lora_B"], dtype=np.float32).transpose(0, 2, 1).reshape(128, D)),
        "gwT": np.ascontiguousarray(np.asarray(inputs["L0_ffn_gate_w"], dtype=np.float32).T),
        "ident": np.eye(128, dtype=np.float32).astype(bf),
        "e816": np.ascontiguousarray(
            (2.0 * np.eye(8, dtype=np.float32)).repeat(16, axis=1)).astype(bf),
    }
    P = np.zeros((128, 128), dtype=np.float32)
    for d in range(64):
        P[d, d + 64] = -1.0
        P[64 + d, d] = 1.0
    shared["protT"] = np.ascontiguousarray(P.T).astype(bf)

    inv = 1.0 / (10000.0 ** (np.arange(0, HD, 2, dtype=np.float32) / HD))
    pos = np.arange(NC * TPC, dtype=np.float32)
    freqs = pos[:, None] * inv[None, :]
    emb = np.concatenate([freqs, freqs], axis=1)
    cos_full = np.cos(emb).astype(np.float32)
    sin_full = np.sin(emb).astype(np.float32)

    tri = np.tril(np.ones((128, 128), dtype=np.float32)).T
    per_core = []
    xr = x.reshape(NC * TPC // 128, 128, D)
    for c in range(NC):
        own = np.arange(NT) * 8 + c
        x0c = np.ascontiguousarray(xr[own].reshape(TPC, D)).astype(np.float32)
        posr = (own[:, None] * 128 + np.arange(128)[None, :]).reshape(-1)
        cosT = np.ascontiguousarray(cos_full[posr].T)
        sinT = np.ascontiguousarray(sin_full[posr].T)
        mask = np.zeros((128, 8, 128), dtype=np.float32)
        for w in range(8):
            if w < c:
                mask[:, w, :] = 1.0
            elif w == c:
                mask[:, w, :] = tri
        d = dict(shared)
        d.update(x0=x0c, cosT=cosT, sinT=sinT,
                 maskcat=np.ascontiguousarray(mask.reshape(128, 1024)).astype(bf))
        per_core.append(d)
    return per_core


def build_nc(debug=(), detect_races=False):
    nc = bass.Bass("TRN2", target_bir_lowering=False, debug=False,
                   detect_race_conditions=detect_races)
    ap = {}

    def din(name, shape, dt):
        ap[name] = nc.dram_tensor(name, shape, dt, kind="ExternalInput").ap()

    din("x0", [TPC, D], F32)
    din("qwT_q", [D, D], BF16)
    din("qwT_k", [1024, D], BF16)
    din("qwT_v", [1024, D], BF16)
    din("qwT_o", [D, D], BF16)
    din("sT_q", [128, 32 * NG], F32)
    din("sT_k", [128, 8 * NG], F32)
    din("sT_v", [128, 8 * NG], F32)
    din("sT_o", [128, 32 * NG], F32)
    din("bwT", [D, D], F32)
    din("bvec", [128, 32], F32)
    din("acatT", [D, 128], F32)
    din("bcatT", [128, D], F32)
    din("gwT", [D, 8], F32)
    din("ident", [128, 128], BF16)
    din("e816", [8, 128], BF16)
    din("protT", [128, 128], BF16)
    din("cosT", [128, TPC], F32)
    din("sinT", [128, TPC], F32)
    din("maskcat", [128, 1024], BF16)

    out_ap = nc.dram_tensor("out", [TPC, D], F32, kind="ExternalOutput").ap()
    dbg = {}
    for name, shape, dt in [
        ("dbg_h1", [128, 4096], BF16),
        ("dbg_contrib", [2048, 1024], BF16),
        ("dbg_qpark", [512, 1024], BF16),
        ("dbg_ctx", [128, 4096], BF16),
        ("dbg_ao", [128, 8192], BF16),
        ("dbg_x1", [TPC, D], BF16),
        ("dbg_h2", [128, 4096], BF16),
        ("dbg_f", [128, 8192], BF16),
        ("dbg_gate", [128, 64], F32),
    ]:
        if name in debug:
            dbg[name] = nc.dram_tensor(name, shape, dt, kind="ExternalOutput").ap()

    qpark = nc.dram_tensor("qpark", [NQH * 128, TPC], BF16).ap()
    x1park = nc.dram_tensor("x1park", [TPC, D], BF16).ap()
    cc_t = {}
    for nm in ("ka", "kb", "va", "vb"):
        cc_t[nm + "_in"] = nc.dram_tensor(f"cc{nm}_in", [512, 1024], BF16)
        cc_t[nm + "_out"] = nc.dram_tensor(f"cc{nm}_out", [8 * 512, 1024], BF16,
                                           addr_space="Shared")
    cc_ap = {k: v.ap() for k, v in cc_t.items()}

    pl = Plan()
    SEMNAMES = ["ld", "ldsc", "ldx0", "ldx1", "lds0", "lds1", "ldp0", "ldp1",
                "ldv0", "ldv1", "ldq0", "ldq1", "ldk0", "ldk1", "gld", "dbs",
                "swl0", "swl1", "gx1",
                "pe", "act", "dve", "gps", "cka", "ckb", "cva", "cvb",
                "sc", "ex", "mk", "av"]
    pl.sem_init(SEMNAMES)

    with ExitStack() as stk:
        e = stk.enter_context
        strips = e(nc.sbuf_tensor("strips", [128, 8192], BF16))
        wdq = e(nc.sbuf_tensor("wdq", [128, 8192], BF16))
        ident = e(nc.sbuf_tensor("identS", [128, 128], BF16))
        protT = e(nc.sbuf_tensor("protTS", [128, 128], BF16))
        e816 = e(nc.sbuf_tensor("e816S", [8, 128], BF16))
        ones128 = e(nc.sbuf_tensor("ones128", [128, 128], BF16))
        negshift = e(nc.sbuf_tensor("negshift", [128, 1], F32))
        epsv = e(nc.sbuf_tensor("epsv", [128, 1], F32))
        bvec_sb = e(nc.sbuf_tensor("bvec_sb", [128, 32], F32))
        bnbuf = e(nc.sbuf_tensor("bnbuf", [128, 96], F32))
        bnagg = e(nc.sbuf_tensor("bnagg", [128, 4], F32))
        varv = e(nc.sbuf_tensor("varv", [128, 2], F32))
        rstd = e(nc.sbuf_tensor("rstd", [128, 2], F32))
        negmr = e(nc.sbuf_tensor("negmr", [128, 2], F32))
        rstd8 = e(nc.sbuf_tensor("rstd8", [128, 8], F32))
        negmr8 = e(nc.sbuf_tensor("negmr8", [128, 8], F32))
        A_h = e(nc.sbuf_tensor("A_h", [128, 32768], BF16))
        A_big = e(nc.sbuf_tensor("A_big", [128, 33024], BF16))
        A_xf = e(nc.sbuf_tensor("A_xf", [128, 8192], F32))
        A_xb = e(nc.sbuf_tensor("A_xb", [128, 6144], BF16))
        g_ltm = e(nc.sbuf_tensor("g_ltm", [128, 64], F32))
        g_m1 = e(nc.sbuf_tensor("g_m1", [128, 8], F32))
        g_m2 = e(nc.sbuf_tensor("g_m2", [128, 8], F32))
        g_eq1 = e(nc.sbuf_tensor("g_eq1", [128, 64], F32))
        g_eq2 = e(nc.sbuf_tensor("g_eq2", [128, 64], F32))
        g_msk = e(nc.sbuf_tensor("g_msk", [128, 64], F32))
        g_exp = e(nc.sbuf_tensor("g_exp", [128, 8], F32))
        g_w1 = e(nc.sbuf_tensor("g_w1", [128, 8], F32))
        g_w2 = e(nc.sbuf_tensor("g_w2", [128, 8], F32))
        g_ctm = e(nc.sbuf_tensor("g_ctm", [128, 64], F32))
        g_ctmb = e(nc.sbuf_tensor("g_ctmb", [128, 64], BF16))

        PA = e(nc.psum_tensor("PA", [128, 1024], F32))
        PB = e(nc.psum_tensor("PB", [128, 1024], F32))
        PC = e(nc.psum_tensor("PC", [128, 1536], F32))
        PD = e(nc.psum_tensor("PD", [128, 512], F32))
        PS = [PA[:, 0:512], PA[:, 512:1024], PB[:, 0:512], PB[:, 512:1024],
              PC[:, 0:512], PC[:, 512:1024]]
        CTXCOL = [(j // 3) * 512 + (j % 3) * 129 for j in range(8)]

        semh = {}
        for n in SEMNAMES:
            semh[n] = e(nc.semaphore(n))
        block = e(nc.Block())

        xslot = lambda i: A_xf[:, 4096 * (i % 2):4096 * (i % 2) + 4096]
        sT_regions = [(A_xf[:, 4096:5120], A_xf[:, 5120:6144]),
                      (A_xf[:, 6144:7168], A_xf[:, 7168:8192])]
        sT_cur = [0]
        bnbig = A_xf[:, 0:1536]
        bnbigD = A_xf[:, 6144:7680]
        h1tm = A_xb[:, 0:4096]
        A_h3 = A_h[:, :].rearrange("p (kc t) -> p kc t", t=1024)
        A_big3 = A_big[:, 0:32768].rearrange("p (a b) -> p a b", b=1024)
        at3 = A_big[:, 0:32768].rearrange("p (tt d) -> p tt d", d=4096)
        x0r = ap["x0"].rearrange("(t p) d -> t p d", p=128)
        x1r = x1park.rearrange("(t p) d -> t p d", p=128)
        outr = out_ap.rearrange("(t p) d -> t p d", p=128)


        def _init_gps(g, s):
            g.memset(negshift[:, :], -SHIFT)
            g.memset(epsv[:, :], EPS)
            g.memset(ones128[:, :], 1.0).then_inc(s["gps"], 1)
        pl.step("gpsimd", _init_gps)
        GPS_INIT = pl.inc("gps")

        def _init_sync(sy, s):
            sy.dma_start(out=ident[:, :], in_=ap["ident"][:, :]).then_inc(s["ld"], 16)
            sy.dma_start(out=protT[:, :], in_=ap["protT"][:, :]).then_inc(s["ld"], 16)
            sy.dma_start(out=e816[:, :], in_=ap["e816"][:, :]).then_inc(s["ld"], 16)
            sy.dma_start(out=bvec_sb[:, :], in_=ap["bvec"][:, :]).then_inc(s["ld"], 16)
        pl.step("sync", _init_sync)
        LD_INIT = pl.inc("ld", 64)

        def _init_act(sc, s):
            sc.add_instruction(mybir.InstLoadActFuncSet(
                name=nc.get_next_instruction_name(), act_func_set_id=6, ins=[], outs=[]))
        pl.step("scalar", _init_act)


        def W(eng, sem, v):
            def f(E, s):
                if v > 0:
                    E.wait_ge(s[sem], v)
            return f

        def seq(*frags):
            def f(E, s):
                for fr in frags:
                    fr(E, s)
            return f

        def plan_ln(src_tile_fn, dst_fn, pre_wait, par, extra_norm_wait=0):
            bb = bnbuf[:, 48 * par:48 * par + 48]
            ba = bnagg[:, 2 * par:2 * par + 2]
            rs = rstd[:, par:par + 1]
            nm = negmr[:, par:par + 1]
            sem0, val0 = pre_wait

            def v_stats(v, s):
                if val0 > 0:
                    v.wait_ge(s[sem0], val0)
                for i in range(8):
                    ins = v.bn_stats(bb[:, 6 * i:6 * i + 6],
                                     src_tile_fn()[:, 512 * i:512 * i + 512])
                ins.then_inc(s["dve"], 1)
            pl.step("vector", v_stats)
            m_stats = pl.inc("dve")

            def g_f(g, s):
                g.wait_ge(s["dve"], m_stats)
                g.engine_nop().then_inc(s["gps"], 1)
            pl.step("gpsimd", g_f)
            m_f = pl.inc("gps")

            def v_aggr(v, s):
                v.wait_ge(s["gps"], m_f)
                v.bn_aggr(ba[:, :], bb[:, :]).then_inc(s["dve"], 1)
            pl.step("vector", v_aggr)
            m_aggr = pl.inc("dve")

            def sc_rstd(sc, s):
                sc.wait_ge(s["dve"], m_aggr)
                sc.activation(out=rs, in_=ba[:, 1:2], func=AF.Ln, bias=epsv[:, 0:1])
                sc.activation(out=rs, in_=rs, func=AF.Exp, scale=-0.5).then_inc(s["act"], 1)
            pl.step("scalar", sc_rstd)
            m_rstd = pl.inc("act")

            def v_negmr(v, s):
                v.wait_ge(s["act"], m_rstd)
                v.tensor_scalar(out=nm, in0=ba[:, 0:1], scalar1=rs, op0=ALU.mult,
                                scalar2=-1.0, op1=ALU.mult).then_inc(s["dve"], 1)
            pl.step("vector", v_negmr)
            m_negmr = pl.inc("dve")

            wnorm = extra_norm_wait

            def sc_norm(sc, s):
                sc.wait_ge(s["dve"], m_negmr)
                if wnorm:
                    wnorm(sc, s)
                sc.activation(out=dst_fn(), in_=src_tile_fn(), func=AF.Identity,
                              scale=rs, bias=nm).then_inc(s["act"], 1)
            pl.step("scalar", sc_norm)
            return pl.inc("act")

        def plan_fence(sem="dve", val=None):
            v = pl.val(sem) if val is None else val

            def g_f(g, s, sem=sem, v=v):
                g.wait_ge(s[sem], v)
                g.engine_nop().then_inc(s["gps"], 1)
            pl.step("gpsimd", g_f)
            return pl.inc("gps")

        def plan_transpose_tile(src_fn, dst_col_fn, wait_act, tp_banks, evac_hist):
            for grp in range(8):
                pb = PS[tp_banks[grp % 2]]
                gi = len(evac_hist) + 2 * 0
                prev = evac_hist[-2] if len(evac_hist) >= 2 else 0
                wa = wait_act if grp == 0 else 0

                def t_tp(t, s, pb=pb, prev=prev, wa=wa, grp=grp, src_fn=src_fn):
                    if wa:
                        t.wait_ge(s["act"], wa)
                    if prev:
                        t.wait_ge(s["dve"], prev)
                    for q in range(4):
                        kc = grp * 4 + q
                        ins = t.matmul(pb[:, 128 * q:128 * q + 128],
                                       src_fn()[:, 128 * kc:128 * kc + 128], ident[:, :],
                                       start=(q == 0), stop=True, skip_group_check=(q != 0))
                    ins.then_inc(s["pe"], 1)
                pl.step("tensor", t_tp)
                m_tp = pl.inc("pe")

                def v_ev(v, s, pb=pb, m_tp=m_tp, grp=grp, dst_col_fn=dst_col_fn):
                    v.wait_ge(s["pe"], m_tp)
                    v.tensor_copy(dst_col_fn(grp),
                                  pb[:, :].rearrange("p (a b) -> p a b", a=4)
                                  ).then_inc(s["dve"], 1)
                pl.step("vector", v_ev)
                evac_hist.append(pl.inc("dve"))
            return evac_hist[-1]

        def plan_ln_batch(pre_dve, pre_sem="dve", width=192, norm_dst=None,
                          norm_wait=None, skip_norms=False, region_fn=None):
            agg8 = bnbuf[:, 0:16]
            if region_fn is None:
                region_fn = lambda tt: bnbig[:, tt * width:tt * width + width]

            def v_ag(v, s, pre_dve=pre_dve, pre_sem=pre_sem, region_fn=region_fn):
                v.wait_ge(s[pre_sem], pre_dve)
                for tt in range(NT):
                    ins = v.bn_aggr(agg8[:, 2 * tt:2 * tt + 2], region_fn(tt))
                ins.then_inc(s["dve"], 1)
            pl.step("vector", v_ag)
            m_ag = pl.inc("dve")
            m_f1 = plan_fence("dve", m_ag)

            def sc_rs(sc, s, m_f1=m_f1):
                sc.wait_ge(s["gps"], m_f1)
                for tt in range(NT):
                    sc.activation(out=rstd8[:, tt:tt + 1], in_=agg8[:, 2 * tt + 1:2 * tt + 2],
                                  func=AF.Ln, bias=epsv[:, 0:1])
                for tt in range(NT):
                    ins = sc.activation(out=rstd8[:, tt:tt + 1], in_=rstd8[:, tt:tt + 1],
                                        func=AF.Exp, scale=-0.5)
                ins.then_inc(s["act"], 1)
            pl.step("scalar", sc_rs)
            m_rs = pl.inc("act")

            def v_nm(v, s, m_rs=m_rs):
                v.wait_ge(s["act"], m_rs)
                for tt in range(NT):
                    ins = v.tensor_scalar(out=negmr8[:, tt:tt + 1], in0=agg8[:, 2 * tt:2 * tt + 1],
                                          scalar1=rstd8[:, tt:tt + 1], op0=ALU.mult,
                                          scalar2=-1.0, op1=ALU.mult)
                ins.then_inc(s["dve"], 1)
            pl.step("vector", v_nm)
            m_nm = pl.inc("dve")
            m_f2 = plan_fence("dve", m_nm)
            norms = []
            if skip_norms:
                return m_f2, m_ag
            for tt in range(NT):
                wfn = norm_wait(tt) if norm_wait else None

                def sc_norm(sc, s, tt=tt, m_f2=m_f2, wfn=wfn):
                    if tt == 0:
                        sc.wait_ge(s["gps"], m_f2)
                    if wfn:
                        wfn(sc, s)
                    dst = norm_dst(tt) if norm_dst else at3[:, tt, :]
                    sc.activation(out=dst, in_=at3[:, tt, :], func=AF.Identity,
                                  scale=rstd8[:, tt:tt + 1], bias=negmr8[:, tt:tt + 1]
                                  ).then_inc(s["act"], 1)
                pl.step("scalar", sc_norm)
                norms.append(pl.inc("act"))
            return norms, m_ag

        slot_free_tp = [0, 0]
        wdq_free = [0, 0]
        deq_evac = []

        def plan_dequant_strip(mat, si, slot, dst, dst_is_abig=False, first_tp_wait=None):
            sl = strips[:, 4096 * slot:4096 * slot + 4096]
            free_at = slot_free_tp[slot]

            lsn = f"lds{slot}"

            def sy(sy_, s, si=si, free_at=free_at, sl=sl, mat=mat, lsn=lsn):
                if free_at:
                    sy_.wait_ge(s["pe"], free_at)
                sy_.dma_start(out=sl, in_=ap[f"qwT_{mat}"][128 * si:128 * si + 128, :]
                              ).then_inc(s[lsn], 16)
            pl.step("sync", sy)
            m_ld = pl.inc(lsn, 16)

            sTt, m8t = sT_regions[sT_cur[0]]

            scl_g = []
            for gblk in range(8):
                def v_scl(v, s, si=si, sl=sl, m_ld=m_ld, gblk=gblk, lsn=lsn,
                          sTt=sTt, m8t=m8t):
                    if gblk == 0:
                        v.wait_ge(s[lsn], m_ld)
                    for g in range(4 * gblk, 4 * gblk + 4):
                        ins = v.tensor_scalar(
                            out=sl[:, 128 * g:128 * g + 128],
                            in0=sl[:, 128 * g:128 * g + 128],
                            scalar1=sTt[:, si * NG + g:si * NG + g + 1], op0=ALU.mult,
                            scalar2=m8t[:, si * NG + g:si * NG + g + 1], op1=ALU.add)
                    ins.then_inc(s["dve"], 1)
                pl.step("vector", v_scl)
                scl_g.append(pl.inc("dve"))
            m_scl = scl_g[-1]
            last_scl_user[sT_cur[0]] = m_scl

            wslot_free = 0 if dst_is_abig else wdq_free[slot]
            deq_pbs = [PS[4], PS[5], PC[:, 1024:1536], PD[:, 0:512]]
            for grp in range(8):
                pb = deq_pbs[len(deq_evac) % 4]
                prev = deq_evac[-4] if len(deq_evac) >= 4 else 0

                def t_tp(t, s, grp=grp, pb=pb, wv=max(prev, scl_g[grp]), sl=sl,
                         wfree=(wslot_free if grp == 0 else 0),
                         ftw=(first_tp_wait if grp == 0 else None)):
                    t.wait_ge(s["dve"], wv)
                    if ftw:
                        t.wait_ge(s[ftw[0]], ftw[1])
                    if wfree:
                        t.wait_ge(s["pe"], wfree)
                    for q in range(4):
                        kc = grp * 4 + q
                        ins = t.matmul(pb[:, 128 * q:128 * q + 128],
                                       sl[:, 128 * kc:128 * kc + 128], ident[:, :],
                                       start=(q == 0), stop=True, skip_group_check=(q != 0))
                    ins.then_inc(s["pe"], 1)
                pl.step("tensor", t_tp)
                m_tp = pl.inc("pe")

                if dst_is_abig:
                    dst_ap = dst(grp)
                else:
                    dst_ap = wdq[:, 4096 * slot + 512 * grp:4096 * slot + 512 * grp + 512
                                 ].rearrange("p (a b) -> p a b", a=4)

                def v_ev(v, s, pb=pb, m_tp=m_tp, dst_ap=dst_ap):
                    v.wait_ge(s["pe"], m_tp)
                    v.tensor_copy(dst_ap, pb[:, :].rearrange("p (a b) -> p a b", a=4)
                                  ).then_inc(s["dve"], 1)
                pl.step("vector", v_ev)
                deq_evac.append(pl.inc("dve"))
            slot_free_tp[slot] = pl.val("pe")
            return pl.val("dve")

        last_scl_user = [0, 0]

        def plan_scales_load(mat, wait_act=0):
            ncols = ap[f"sT_{mat}"].tensor.shape[1]
            sT_cur[0] ^= 1
            reg = sT_cur[0]
            sTt, m8t = sT_regions[reg]
            wprev = last_scl_user[reg]

            def sy(sy_, s, ncols=ncols, mat=mat, wait_act=wait_act, wprev=wprev, sTt=sTt):
                if wait_act:
                    sy_.wait_ge(s["act"], wait_act)
                if wprev:
                    sy_.wait_ge(s["dve"], wprev)
                sy_.dma_start(out=sTt[:, 0:ncols], in_=ap[f"sT_{mat}"][:, :]
                              ).then_inc(s["ldsc"], 16)
            pl.step("sync", sy)
            m = pl.inc("ldsc", 16)

            def v_m8(v, s, ncols=ncols, m=m, sTt=sTt, m8t=m8t):
                v.wait_ge(s["ldsc"], m)
                v.tensor_scalar(out=m8t[:, 0:ncols], in0=sTt[:, 0:ncols], scalar1=-8.0,
                                op0=ALU.mult, scalar2=None).then_inc(s["dve"], 1)
            pl.step("vector", v_m8)
            m_m8 = pl.inc("dve")
            m_fence = plan_fence("dve", m_m8)

            def v_wait_f(v, s, m_fence=m_fence):
                v.wait_ge(s["gps"], m_fence)
            pl.step("vector", v_wait_f)
            return m_m8

        A_chains = ["gld", "swl0", "swl1"]
        A_xld = []
        A_stats_f = []
        A_norm = [0] * NT
        A_tp_done = [0] * NT
        tp_hist_A = []
        A0_SPLIT = [(0, 1536), (1536, 3072), (3072, 4096)]
        for tt in range(NT):
            if tt == 0:
                def sy0(sy_, s):
                    for (a, b), ch in zip(A0_SPLIT, A_chains):
                        sy_.dma_start(out=at3[:, 0, a:b], in_=x0r[0][:, a:b]
                                      ).then_inc(s[ch], 16)
                pl.step("gpsimd", sy0)
                A_xld.append([(ch, pl.inc(ch, 16)) for ch in A_chains])
                continue
            ch = A_chains[tt % 3]

            def sy(sy_, s, tt=tt, ch=ch):
                sy_.dma_start(out=at3[:, tt, :], in_=x0r[tt]).then_inc(s[ch], 16)
            pl.step("gpsimd", sy)
            A_xld.append([(ch, pl.inc(ch, 16))])

        def plan_a_front(tt):
            bbuf = bnbuf[:, 48 * (tt % 2):48 * (tt % 2) + 48]

            def v_st(v, s, tt=tt, bbuf=bbuf):
                for ch, val in A_xld[tt]:
                    v.wait_ge(s[ch], val)
                for i in range(8):
                    ins = v.bn_stats(bbuf[:, 6 * i:6 * i + 6],
                                     at3[:, tt, 512 * i:512 * i + 512])
                ins.then_inc(s["dve"], 1)
            pl.step("vector", v_st)
            m_st = pl.inc("dve")
            A_stats_f.append(plan_fence("dve", m_st))

        def plan_a_mid(tt):
            par = tt % 2
            ba = bnagg[:, 2 * par:2 * par + 2]
            rs = rstd[:, par:par + 1]
            nm = negmr[:, par:par + 1]

            def v_aggr(v, s, tt=tt, ba=ba, par=par):
                v.wait_ge(s["gps"], A_stats_f[tt])
                v.bn_aggr(ba, bnbuf[:, 48 * par:48 * par + 48]).then_inc(s["dve"], 1)
            pl.step("vector", v_aggr)
            m_aggr = pl.inc("dve")

            def sc_rstd(sc, s, ba=ba, rs=rs, m_aggr=m_aggr):
                sc.wait_ge(s["dve"], m_aggr)
                sc.activation(out=rs, in_=ba[:, 1:2], func=AF.Ln, bias=epsv[:, 0:1])
                sc.activation(out=rs, in_=rs, func=AF.Exp, scale=-0.5
                              ).then_inc(s["act"], 1)
            pl.step("scalar", sc_rstd)
            m_rstd = pl.inc("act")

            def v_negmr(v, s, ba=ba, rs=rs, nm=nm, m_rstd=m_rstd):
                v.wait_ge(s["act"], m_rstd)
                v.tensor_scalar(out=nm, in0=ba[:, 0:1], scalar1=rs, op0=ALU.mult,
                                scalar2=-1.0, op1=ALU.mult).then_inc(s["dve"], 1)
            pl.step("vector", v_negmr)
            m_negmr = pl.inc("dve")

            h1tm_free = A_tp_done[tt - 1] if tt >= 1 else 0

            def sc_norm(sc, s, tt=tt, rs=rs, nm=nm, m_negmr=m_negmr,
                        h1tm_free=h1tm_free):
                sc.wait_ge(s["dve"], m_negmr)
                if h1tm_free:
                    sc.wait_ge(s["pe"], h1tm_free)
                sc.activation(out=h1tm, in_=at3[:, tt, :], func=AF.Identity,
                              scale=rs, bias=nm).then_inc(s["act"], 1)
            pl.step("scalar", sc_norm)
            A_norm[tt] = pl.inc("act")
            plan_transpose_tile(
                src_fn=(lambda: h1tm),
                dst_col_fn=(lambda grp, tt=tt: A_h3[:, grp * 4:grp * 4 + 4,
                                                    tt * 128:tt * 128 + 128]),
                wait_act=A_norm[tt], tp_banks=(0, 1), evac_hist=tp_hist_A)
            A_tp_done[tt] = pl.val("pe")

        for tt in range(NT):
            plan_a_front(tt)
            if tt >= 1:
                plan_a_mid(tt - 1)
        plan_a_mid(NT - 1)
        H1_DONE = tp_hist_A[-1]


        cosS = A_xf[:, 0:1024]
        sinS = A_xf[:, 1024:2048]
        tmpro = lambda i: A_xf[:, 2048 + 1024 * (i % 2):2048 + 1024 * (i % 2) + 1024]
        qbfS = lambda i: A_xb[:, 1024 * (i % 2):1024 * (i % 2) + 1024]
        qrS = lambda i: A_xb[:, 2048 + 1024 * (i % 2):2048 + 1024 * (i % 2) + 1024]

        def sy_cs(sy_, s):
            sy_.dma_start(out=cosS, in_=ap["cosT"][:, :]).then_inc(s["ld"], 16)
            sy_.dma_start(out=sinS, in_=ap["sinT"][:, :]).then_inc(s["ld"], 16)
        pl.step("sync", sy_cs)
        CS_LD = pl.inc("ld", 32)

        def plan_proj_rope(mat, n_heads, park, post_head=None):
            st_hist = []
            ev_hist = []
            rot_hist = []
            t3_hist = []
            m8 = plan_scales_load(mat, wait_act=0)

            def v_wait_m8(v, s, m8=m8):
                v.wait_ge(s["dve"], m8)
            deq_m = {0: plan_dequant_strip(mat, 0, 0, None)}
            for hh in range(n_heads):
                slot = hh % 2
                m_deq = deq_m[hh]

                def t_proj(t, s, slot=slot, m_deq=m_deq, hh=hh):
                    t.wait_ge(s["dve"], m_deq)
                    if hh == 0:
                        t.wait_ge(s["dve"], H1_DONE)
                    for half in range(2):
                        for kc in range(NG):
                            ins = t.matmul(
                                PA[:, half * 512:half * 512 + 512],
                                wdq[:, 4096 * slot + 128 * kc:4096 * slot + 128 * kc + 128],
                                A_h3[:, kc, half * 512:half * 512 + 512],
                                start=(kc == 0), stop=(kc == NG - 1))
                    ins.then_inc(s["pe"], 1)
                pl.step("tensor", t_proj)
                m_proj = pl.inc("pe")
                wdq_free[slot] = m_proj
                if hh + 1 < n_heads:
                    deq_m[hh + 1] = plan_dequant_strip(mat, hh + 1, (hh + 1) % 2, None)

                qbf_free = rot_hist[-2] if len(rot_hist) >= 2 else 0

                def sc_ev(sc, s, hh=hh, m_proj=m_proj, qbf_free=qbf_free):
                    sc.wait_ge(s["pe"], m_proj)
                    if qbf_free:
                        sc.wait_ge(s["pe"], qbf_free)
                    sc.activation(out=qbfS(hh)[:, :], in_=PA[:, :],
                                  func=AF.Identity).then_inc(s["act"], 1)
                pl.step("scalar", sc_ev)
                m_ev = pl.inc("act")
                ev_hist.append(m_ev)

                rot_free = t3_hist[-1] if t3_hist else 0

                def t_rot(t, s, hh=hh, m_ev=m_ev, rot_free=rot_free):
                    t.wait_ge(s["act"], m_ev)
                    if rot_free:
                        t.wait_ge(s["dve"], rot_free)
                    for half in range(2):
                        ins = t.matmul(PB[:, half * 512:half * 512 + 512], protT[:, :],
                                       qbfS(hh)[:, half * 512:half * 512 + 512],
                                       start=True, stop=True)
                    ins.then_inc(s["pe"], 1)
                pl.step("tensor", t_rot)
                m_rot = pl.inc("pe")
                rot_hist.append(m_rot)

                tmp_free = t3_hist[-2] if len(t3_hist) >= 2 else 0

                def g_t1(g, s, hh=hh, m_ev=m_ev, tmp_free=tmp_free):
                    g.wait_ge(s["act"], m_ev)
                    if hh == 0:
                        g.wait_ge(s["ld"], CS_LD)
                    if tmp_free:
                        g.wait_ge(s["dve"], tmp_free)
                    g.tensor_tensor(out=tmpro(hh), in0=qbfS(hh)[:, :], in1=cosS,
                                    op=ALU.mult).then_inc(s["gps"], 1)
                pl.step("gpsimd", g_t1)
                m_t1 = pl.inc("gps")

                qr_free = st_hist[-2] if len(st_hist) >= 2 else None

                def v_ropes(v, s, hh=hh, m_rot=m_rot, m_t1=m_t1, qr_free=qr_free):
                    v.wait_ge(s["pe"], m_rot)
                    v.tensor_tensor(out=PB[:, :], in0=PB[:, :], in1=sinS[:, :],
                                    op=ALU.mult)
                    v.wait_ge(s["gps"], m_t1)
                    if qr_free:
                        v.wait_ge(s[qr_free[0]], qr_free[1])
                    v.tensor_tensor(out=qrS(hh)[:, :], in0=PB[:, :],
                                    in1=tmpro(hh)[:, :], op=ALU.add
                                    ).then_inc(s["dve"], 1)
                pl.step("vector", v_ropes)
                m_t3 = pl.inc("dve")
                t3_hist.append(m_t3)

                lpn = f"ldp{hh % 2}"

                def sy_st(sy_, s, hh=hh, m_t3=m_t3, park=park, lpn=lpn):
                    sy_.wait_ge(s["dve"], m_t3)
                    if park:
                        dst = qpark[hh * 128:hh * 128 + 128, :]
                    else:
                        half = "ka" if hh < 4 else "kb"
                        dst = cc_ap[half + "_in"][(hh % 4) * 128:(hh % 4) * 128 + 128, :]
                    sy_.dma_start(out=dst, in_=qrS(hh)[:, :]).then_inc(s[lpn], 16)
                pl.step("sync", sy_st)
                st_hist.append((lpn, pl.inc(lpn, 16)))
                if post_head is not None:
                    post_head(hh)
            return (pl.val("ldp0"), pl.val("ldp1"))

        Q_PARKED = plan_proj_rope("q", NQH, park=True)

        m8v = plan_scales_load("v")
        wv3 = A_big[:, 0:32768].rearrange("p (kc sd) -> p kc sd", sd=1024)
        for si in range(8):
            plan_dequant_strip(
                "v", si, si % 2,
                dst=(lambda grp, si=si: wv3[:, grp * 4:grp * 4 + 4,
                                            si * 128:si * 128 + 128]),
                dst_is_abig=True)
        WV_DONE = pl.val("dve")
        vbf = lambda i: A_xb[:, 4096 + 1024 * (i % 2):4096 + 1024 * (i % 2) + 1024]
        v_ev_hist = []
        v_st_hist = []
        for ts_ in range(NT):
            wev_prev = v_ev_hist[-1] if v_ev_hist else 0

            def t_vp(t, s, ts_=ts_, wev_prev=wev_prev):
                if ts_ == 0:
                    t.wait_ge(s["dve"], WV_DONE)
                    t.wait_ge(s["dve"], H1_DONE)
                if ts_ >= 1:
                    t.wait_ge(s["act"], wev_prev)
                for half in range(2):
                    for kc in range(NG):
                        ins = t.matmul(PA[:, half * 512:half * 512 + 512],
                                       A_h3[:, kc, ts_ * 128:ts_ * 128 + 128],
                                       wv3[:, kc, half * 512:half * 512 + 512],
                                       start=(kc == 0), stop=(kc == NG - 1))
                ins.then_inc(s["pe"], 1)
            pl.step("tensor", t_vp)
            m_vp = pl.inc("pe")

            vst_free = v_st_hist[-2] if len(v_st_hist) >= 2 else None

            def sc_vev(sc, s, ts_=ts_, m_vp=m_vp, vst_free=vst_free):
                sc.wait_ge(s["pe"], m_vp)
                if vst_free:
                    sc.wait_ge(s[vst_free[0]], vst_free[1])
                sc.activation(out=vbf(ts_)[:, :], in_=PA[:, :],
                              func=AF.Identity).then_inc(s["act"], 1)
            pl.step("scalar", sc_vev)
            m_vev = pl.inc("act")
            v_ev_hist.append(m_vev)

            lvn = f"ldv{ts_ % 2}"

            def sy_vst(sy_, s, ts_=ts_, m_vev=m_vev, lvn=lvn):
                sy_.wait_ge(s["act"], m_vev)
                half = "va" if ts_ < 4 else "vb"
                sy_.dma_start(out=cc_ap[half + "_in"][(ts_ % 4) * 128:(ts_ % 4) * 128 + 128, :],
                              in_=vbf(ts_)[:, :]).then_inc(s[lvn], 16)
            pl.step("sync", sy_vst)
            v_st_hist.append((lvn, pl.inc(lvn, 16)))
            if ts_ == 3:
                VA0, VA1 = pl.val("ldv0"), pl.val("ldv1")

                def g_ccva(g, s, VA0=VA0, VA1=VA1):
                    g.wait_ge(s["ldv0"], VA0)
                    g.wait_ge(s["ldv1"], VA1)
                    g.collective_compute(
                        "AllGather", ALU.bypass, replica_groups=[list(range(NC))],
                        ins=[cc_t["va_in"].ap().opt()], outs=[cc_t["va_out"].ap().opt()]
                    ).then_inc(s["cva"], 1)
                pl.step("gpsimd", g_ccva)
                CCVA_DONE = pl.inc("cva")
            if ts_ == 7:
                VB0, VB1 = pl.val("ldv0"), pl.val("ldv1")

                def g_ccvb(g, s, VB0=VB0, VB1=VB1):
                    g.wait_ge(s["ldv0"], VB0)
                    g.wait_ge(s["ldv1"], VB1)
                    g.collective_compute(
                        "AllGather", ALU.bypass, replica_groups=[list(range(NC))],
                        ins=[cc_t["vb_in"].ap().opt()], outs=[cc_t["vb_out"].ap().opt()]
                    ).then_inc(s["cvb"], 1)
                pl.step("gpsimd", g_ccvb)
                CCVB_DONE = pl.inc("cvb")
        V_CONTRIB = (pl.val("ldv0"), pl.val("ldv1"))
        V_PROJ_DONE = pl.val("pe")


        CCK_M = {}

        def k_hook(hh):
            if hh == 3 or hh == 7:
                half = "ka" if hh == 3 else "kb"
                semn = "cka" if hh == 3 else "ckb"
                P0, P1 = pl.val("ldp0"), pl.val("ldp1")

                def g_cck(g, s, P0=P0, P1=P1, half=half, semn=semn):
                    g.wait_ge(s["ldp0"], P0)
                    g.wait_ge(s["ldp1"], P1)
                    g.collective_compute(
                        "AllGather", ALU.bypass, replica_groups=[list(range(NC))],
                        ins=[cc_t[half + "_in"].ap().opt()],
                        outs=[cc_t[half + "_out"].ap().opt()]).then_inc(s[semn], 1)
                pl.step("gpsimd", g_cck)
                CCK_M[half] = pl.inc(semn)

        K_CONTRIB = plan_proj_rope("k", NKV, park=False, post_head=k_hook)

        def g_ones(g, s):
            g.wait_ge(s["pe"], V_PROJ_DONE)
            for p in range(2):
                vv = A_big[:, 16384 + 8256 * p:16384 + 8256 * p + 8256]
                g.memset(vv.rearrange("p (s c) -> p s c", c=129)[:, :, 128:129], 1.0)
            g.engine_nop().then_inc(s["gps"], 1)
        pl.step("gpsimd", g_ones)
        ONES_DONE = pl.inc("gps")

        Kbuf = lambda p: A_big[:, 8192 * p:8192 * p + 8192]
        Vbuf = lambda p: A_big[:, 16384 + 8256 * p:16384 + 8256 * p + 8256]
        probsS = lambda p: A_xb[:, 1024 * p:1024 * p + 1024]
        qbufS2 = lambda p: A_xb[:, 2048 + 1024 * p:2048 + 1024 * p + 1024]
        maskS = A_xb[:, 4096:5120]
        ctxtm = A_xb[:, 5120:6144]
        rden_v = A_xf[:, 0:8]
        grKA = cc_ap["ka_out"].rearrange("(r x) c -> r x c", x=512)
        grKB = cc_ap["kb_out"].rearrange("(r x) c -> r x c", x=512)
        grVA = cc_ap["va_out"].rearrange("(r x) c -> r x c", x=512)
        grVB = cc_ap["vb_out"].rearrange("(r x) c -> r x c", x=512)

        def sy_mask(sy_, s):
            sy_.wait_ge(s["cka"], CCK_M["ka"])
            sy_.dma_start(out=maskS, in_=ap["maskcat"][:, :]).then_inc(s["ld"], 16)
        pl.step("sync", sy_mask)
        MASK_LD = pl.inc("ld", 16)

        batches = []
        for m in range(4):
            for bb in range(4 * m + 4):
                shared = bb <= 4 * m + 1
                kts = [4 * bb + i for i in range(4)]
                msk = None
                if bb >= 4 * m:
                    msk = "A" if (bb - 4 * m) % 2 == 0 else "B"
                batches.append((m, bb, shared, kts, msk))
        NB = len(batches)

        kvK_ld = []
        kvVA_ld = []
        kvVB_ld = []
        CTPEV = []
        head_av = []
        pend_tp = []
        last_ctp = [0]

        def plan_pending_tp():
            while pend_tp:
                qi0, m_ctxev0 = pend_tp.pop(0)
                for grp in range(2):
                    pd_free = CTPEV[-1] if CTPEV else 0

                    def t_ctp(t, s, qi=qi0, grp=grp, m_ctxev=m_ctxev0, pd_free=pd_free):
                        if grp == 0:
                            t.wait_ge(s["dve"], m_ctxev)
                        if pd_free:
                            t.wait_ge(s["dve"], pd_free)
                        for q4 in range(4):
                            j = grp * 4 + q4
                            ins = t.matmul(PD[:, 128 * q4:128 * q4 + 128],
                                           ctxtm[:, 128 * j:128 * j + 128], ident[:, :],
                                           start=(q4 == 0), stop=True,
                                           skip_group_check=(q4 != 0))
                        ins.then_inc(s["pe"], 1)
                    pl.step("tensor", t_ctp)
                    m_ctp = pl.inc("pe")
                    last_ctp[0] = m_ctp

                    def v_ctpev(v, s, qi=qi0, grp=grp, m_ctp=m_ctp):
                        v.wait_ge(s["pe"], m_ctp)
                        v.tensor_copy(
                            A_h[:, 1024 * qi + 512 * grp:1024 * qi + 512 * grp + 512],
                            PD[:, :]).then_inc(s["dve"], 1)
                    pl.step("vector", v_ctpev)
                    CTPEV.append(pl.inc("dve"))

        qh_sc = []
        evac_m = []
        exp_m = []
        mask_m = {}
        sc_m = []

        for h in range(NKV):
            p = h % 2
            kv_free = head_av[h - 2] if h >= 2 else 0
            lkn = f"ldk{p}"
            grk = grKA if h < 4 else grKB
            cksem = "cka" if h < 4 else "ckb"
            ckm = ("ka" if h < 4 else "kb")

            def sy_k(sy_, s, h=h, p=p, kv_free=kv_free, lkn=lkn, grk=grk,
                     cksem=cksem, ckm=ckm):
                sy_.wait_ge(s[cksem], CCK_M[ckm])
                if kv_free:
                    sy_.wait_ge(s["av"], kv_free)
                for r in range(NC):
                    sy_.dma_start(out=Kbuf(p)[:, 1024 * r:1024 * r + 1024],
                                  in_=grk[r, (h % 4) * 128:(h % 4) * 128 + 128, :]
                                  ).then_inc(s[lkn], 16)
            pl.step("sync", sy_k)
            kvK_ld.append((lkn, pl.inc(lkn, 16 * 8)))

            def sy_va(sy_, s, h=h, p=p, lkn=lkn):
                sy_.wait_ge(s["cva"], CCVA_DONE)
                for r in range(NC):
                    srcv = grVA[r, :, h * 128:h * 128 + 128]
                    srcv = srcv.rearrange("(lt ki) dv -> ki lt dv", ki=128)
                    dstv = Vbuf(p)[:, 1032 * r:1032 * r + 516].rearrange(
                        "p (lt c) -> p lt c", c=129)[:, :, 0:128]
                    sy_.dma_start(out=dstv, in_=srcv).then_inc(s[lkn], 16)
            pl.step("sync", sy_va)
            kvVA_ld.append((lkn, pl.inc(lkn, 16 * 8)))

            def sy_vb(sy_, s, h=h, p=p, lkn=lkn):
                sy_.wait_ge(s["cvb"], CCVB_DONE)
                for r in range(NC):
                    srcv = grVB[r, :, h * 128:h * 128 + 128]
                    srcv = srcv.rearrange("(lt ki) dv -> ki lt dv", ki=128)
                    dstv = Vbuf(p)[:, 1032 * r + 516:1032 * r + 1032].rearrange(
                        "p (lt c) -> p lt c", c=129)[:, :, 0:128]
                    sy_.dma_start(out=dstv, in_=srcv).then_inc(s[lkn], 16)
            pl.step("sync", sy_vb)
            kvVB_ld.append((lkn, pl.inc(lkn, 16 * 8)))

            for qh in range(4):
                qi = h * 4 + qh
                qb_free = qh_sc[qi - 2] if qi >= 2 else 0
                lqn = f"ldq{qi % 2}"

                def sy_q(sy_, s, qi=qi, qb_free=qb_free, lqn=lqn):
                    if qb_free:
                        sy_.wait_ge(s["sc"], qb_free)
                    sy_.dma_start(out=qbufS2(qi % 2), in_=qpark[qi * 128:qi * 128 + 128, :]
                                  ).then_inc(s[lqn], 16)
                pl.step("sync", sy_q)
                q_ld = (lqn, pl.inc(lqn, 16))

                ctx_free = evac_m[qi - 1] if qi >= 1 else 0
                vb_waited = [False]
                av_mv = {}

                def plan_av(n):
                    m, bb, shared, kts, msk = batches[n]
                    gi = qi * NB + n
                    need_b = kts[-1] >= 32 and not vb_waited[0]
                    if need_b:
                        vb_waited[0] = True

                    def t_av(t, s, h=h, p=p, qi=qi, n=n, m=m, bb=bb, shared=shared,
                             kts=tuple(kts), msk=msk, gi=gi, ctx_free=ctx_free,
                             need_b=need_b):
                        if n == 0:
                            t.wait_ge(s[kvVA_ld[h][0]], kvVA_ld[h][1])
                            if ctx_free:
                                t.wait_ge(s["dve"], ctx_free)
                            if qi == 0:
                                t.wait_ge(s["gps"], ONES_DONE)
                        if need_b:
                            t.wait_ge(s[kvVB_ld[h][0]], kvVB_ld[h][1])
                        pbuf = probsS(n % 2)
                        j0, j1 = 2 * m, 2 * m + 1
                        if shared and msk is not None and not (m == 0 and bb == 0):
                            t.wait_ge(s["ex"], exp_m[gi])
                            for kk in range(4):
                                kt = kts[kk]
                                slot = (kt % 8) * 8 + kt // 8
                                stt = (bb == 0 and kk == 0 and j1 == 3)
                                t.matmul(PC[:, CTXCOL[j1]:CTXCOL[j1] + 129],
                                         pbuf[:, 256 * kk + 128:256 * kk + 256],
                                         Vbuf(p)[:, 129 * slot:129 * slot + 129],
                                         start=stt, stop=True, skip_group_check=not stt)
                            t.wait_ge(s["mk"], mask_m[gi])
                            for kk in range(4):
                                kt = kts[kk]
                                slot = (kt % 8) * 8 + kt // 8
                                stt = (bb == 0 and kk == 0 and (j0 == 0 or (j0 == 6 and j0 == 2 * m)))
                                ins = t.matmul(PC[:, CTXCOL[j0]:CTXCOL[j0] + 129],
                                               pbuf[:, 256 * kk:256 * kk + 128],
                                               Vbuf(p)[:, 129 * slot:129 * slot + 129],
                                               start=stt, stop=True,
                                               skip_group_check=not stt)
                        else:
                            if msk is not None:
                                t.wait_ge(s["mk"], mask_m[gi])
                            else:
                                t.wait_ge(s["ex"], exp_m[gi])
                            for kk in range(4):
                                kt = kts[kk]
                                slot = (kt % 8) * 8 + kt // 8
                                if shared:
                                    pairs = [(j0, 256 * kk), (j1, 256 * kk + 128)]
                                else:
                                    pairs = [(j1, 128 * kk)]
                                for (j, pcol) in pairs:
                                    stt = (bb == 0 and kk == 0 and
                                           ((j == 0) or (j == 3) or (j == 6 and j == 2 * m)))
                                    ins = t.matmul(
                                        PC[:, CTXCOL[j]:CTXCOL[j] + 129],
                                        pbuf[:, pcol:pcol + 128],
                                        Vbuf(p)[:, 129 * slot:129 * slot + 129],
                                        start=stt, stop=True,
                                        skip_group_check=not stt)
                        ins.then_inc(s["av"], 1)
                    pl.step("tensor", t_av)
                    av_mv[n] = pl.inc("av")

                def plan_sc(n):
                    m, bb, shared, kts, msk = batches[n]
                    gi = qi * NB + n
                    S = PA if n % 2 == 0 else PB
                    wexp = exp_m[gi - 2] if gi >= 2 else 0

                    def t_sc(t, s, h=h, p=p, qi=qi, n=n, m=m, bb=bb, shared=shared,
                             kts=tuple(kts), S=S, wexp=wexp, q_ld=q_ld):
                        if n == 0:
                            t.wait_ge(s[kvK_ld[h][0]], kvK_ld[h][1])
                            t.wait_ge(s[q_ld[0]], q_ld[1])
                            if qi == 0:
                                t.wait_ge(s["ld"], MASK_LD)
                        if wexp:
                            t.wait_ge(s["ex"], wexp)
                        for kk in range(4):
                            kt = kts[kk]
                            slot = (kt % 8) * 8 + kt // 8
                            if shared:
                                out = S[:, 256 * kk:256 * kk + 256]
                                rhs = qbufS2(qi % 2)[:, 256 * m:256 * m + 256]
                                stt = (kk % 2 == 0)
                            else:
                                out = S[:, 128 * kk:128 * kk + 128]
                                rhs = qbufS2(qi % 2)[:, 128 * (2 * m + 1):
                                                     128 * (2 * m + 1) + 128]
                                stt = (kk == 0)
                            ins = t.matmul(out, Kbuf(p)[:, 128 * slot:128 * slot + 128],
                                           rhs, start=stt, stop=True,
                                           skip_group_check=not stt)
                        ins.then_inc(s["sc"], 1)
                    pl.step("tensor", t_sc)
                    sc_m.append(pl.inc("sc"))

                def plan_exp(n, wait_av=None):
                    m, bb, shared, kts, msk = batches[n]
                    gi = qi * NB + n
                    S = PA if n % 2 == 0 else PB

                    wav = av_mv[wait_av] if wait_av is not None else None

                    def sc_exp(sc, s, gi=gi, n=n, S=S, shared=shared, wav=wav):
                        sc.wait_ge(s["sc"], sc_m[gi])
                        if wav is not None:
                            sc.wait_ge(s["av"], wav)
                        w = 1024 if shared else 512
                        sc.activation(out=probsS(n % 2)[:, 0:w], in_=S[:, 0:w],
                                      func=AF.Exp, scale=ISQ, bias=negshift[:, 0:1]
                                      ).then_inc(s["ex"], 1)
                    pl.step("scalar", sc_exp)
                    exp_m.append(pl.inc("ex"))

                    if msk is not None:
                        def v_mask(v, s, gi=gi, n=n, shared=shared, msk=msk):
                            v.wait_ge(s["ex"], exp_m[gi])
                            mk_half = maskS[:, 0:512] if msk == "A" else maskS[:, 512:1024]
                            pbuf = probsS(n % 2)
                            if shared:
                                v.tensor_tensor(
                                    out=pbuf.rearrange("p (kk two q) -> p kk two q",
                                                       two=2, q=128)[:, :, 0, :],
                                    in0=pbuf.rearrange("p (kk two q) -> p kk two q",
                                                       two=2, q=128)[:, :, 0, :],
                                    in1=mk_half.rearrange("p (kk q) -> p kk q", q=128),
                                    op=ALU.mult).then_inc(s["mk"], 1)
                            else:
                                v.tensor_tensor(out=pbuf[:, 0:512], in0=pbuf[:, 0:512],
                                                in1=mk_half, op=ALU.mult
                                                ).then_inc(s["mk"], 1)
                        pl.step("vector", v_mask)
                        mask_m[gi] = pl.inc("mk")

                for n in range(0, NB, 2):
                    plan_sc(n)
                    plan_exp(n)
                    plan_sc(n + 1)
                    if n >= 2:
                        plan_av(n - 1)
                    plan_exp(n + 1, wait_av=(n - 1 if n >= 2 else None))
                    plan_av(n)
                    if n == 2:
                        plan_pending_tp()
                plan_av(NB - 1)

                qh_sc.append(pl.val("sc"))
                m_av_qh = pl.val("av")

                def v_recip(v, s, m_av_qh=m_av_qh):
                    v.wait_ge(s["av"], m_av_qh)
                    for j in range(8):
                        ins = v.reciprocal(rden_v[:, j:j + 1],
                                           PC[:, CTXCOL[j] + 128:CTXCOL[j] + 129])
                    ins.then_inc(s["dve"], 1)
                pl.step("vector", v_recip)
                m_recip = pl.inc("dve")
                m_rf = plan_fence("dve", m_recip)
                ctm_free = last_ctp[0]

                def v_ctxev(v, s, m_rf=m_rf, ctm_free=ctm_free):
                    v.wait_ge(s["gps"], m_rf)
                    if ctm_free:
                        v.wait_ge(s["pe"], ctm_free)
                    for j in range(8):
                        ins = v.tensor_scalar(
                            out=ctxtm[:, 128 * j:128 * j + 128],
                            in0=PC[:, CTXCOL[j]:CTXCOL[j] + 128],
                            scalar1=rden_v[:, j:j + 1], op0=ALU.mult, scalar2=None)
                    ins.then_inc(s["dve"], 1)
                pl.step("vector", v_ctxev)
                m_ctxev = pl.inc("dve")
                pend_tp.append((qi, m_ctxev))
                evac_m.append(m_ctxev)
            head_av.append(pl.val("av"))
        plan_pending_tp()
        CTX_DONE = CTPEV[-1]

        m8o = plan_scales_load("o")
        aofm = lambda i: A_xb[:, 1024 * (i % 2):1024 * (i % 2) + 1024]
        dp_ev = []
        dp_tp = []
        dp_tevac = []
        deq_mo = {0: plan_dequant_strip("o", 0, 0, None,
                                        first_tp_wait=("dve", CTX_DONE))}
        for oc in range(32):
            m_deq = deq_mo[oc]

            def t_op(t, s, oc=oc, m_deq=m_deq,
                     ev_prev=(dp_ev[oc - 1] if oc >= 1 else 0)):
                t.wait_ge(s["dve"], m_deq)
                if ev_prev:
                    t.wait_ge(s["act"], ev_prev)
                if oc == 0:
                    t.wait_ge(s["dve"], CTX_DONE)
                for half in range(2):
                    for cc in range(NG):
                        ins = t.matmul(
                            PA[:, half * 512:half * 512 + 512],
                            wdq[:, 4096 * (oc % 2) + 128 * cc:4096 * (oc % 2) + 128 * cc + 128],
                            A_h3[:, cc, half * 512:half * 512 + 512],
                            start=(cc == 0), stop=(cc == NG - 1))
                ins.then_inc(s["pe"], 1)
            pl.step("tensor", t_op)
            m_op = pl.inc("pe")
            wdq_free[oc % 2] = m_op
            if oc + 1 < 32:
                deq_mo[oc + 1] = plan_dequant_strip("o", oc + 1, (oc + 1) % 2, None)
            if oc == 2:
                def sy_x00(sy_, s):
                    sy_.wait_ge(s["dve"], CTX_DONE)
                    sy_.dma_start(out=xslot(0)[:, 0:2048],
                                  in_=x0r[0][:, 0:2048]).then_inc(s["ldx0"], 16)
                    sy_.dma_start(out=xslot(0)[:, 2048:4096],
                                  in_=x0r[0][:, 2048:4096]).then_inc(s["ldp0"], 16)
                pl.step("sync", sy_x00)
                E_X00 = (("ldx0", pl.inc("ldx0", 16)), ("ldp0", pl.inc("ldp0", 16)))

            def sc_oev(sc, s, oc=oc, m_op=m_op, ao_free=(dp_tp[2 * (oc - 2) + 1] if oc >= 2 else 0)):
                sc.wait_ge(s["pe"], m_op)
                if ao_free:
                    sc.wait_ge(s["pe"], ao_free)
                sc.activation(out=aofm(oc)[:, 0:512], in_=PS[0][:, :], func=AF.Identity)
                sc.activation(out=aofm(oc)[:, 512:1024], in_=PS[1][:, :],
                              func=AF.Identity).then_inc(s["act"], 1)
            pl.step("scalar", sc_oev)
            m_oev = pl.inc("act")
            dp_ev.append(m_oev)

            def plan_o_tp(oc2):
                m_oev2 = dp_ev[oc2]
                for grp in range(2):
                    pb = PS[2 + grp]
                    prev = dp_tevac[-2] if len(dp_tevac) >= 2 else 0

                    def t_tp(t, s, oc2=oc2, grp=grp, pb=pb, prev=prev,
                             m_oev2=m_oev2):
                        if grp == 0:
                            t.wait_ge(s["act"], m_oev2)
                        if prev:
                            t.wait_ge(s["dve"], prev)
                        for q in range(4):
                            ts_ = grp * 4 + q
                            ins = t.matmul(pb[:, 128 * q:128 * q + 128],
                                           aofm(oc2)[:, 128 * ts_:128 * ts_ + 128],
                                           ident[:, :], start=(q == 0), stop=True,
                                           skip_group_check=(q != 0))
                        ins.then_inc(s["pe"], 1)
                    pl.step("tensor", t_tp)
                    m_tp = pl.inc("pe")
                    dp_tp.append(m_tp)

                    def v_ev(v, s, oc2=oc2, grp=grp, pb=pb, m_tp=m_tp):
                        v.wait_ge(s["pe"], m_tp)
                        v.tensor_copy(at3[:, grp * 4:grp * 4 + 4,
                                          oc2 * 128:oc2 * 128 + 128],
                                      pb[:, :].rearrange("p (a b) -> p a b", a=4))
                        for q4 in range(4):
                            tt = grp * 4 + q4
                            ins = v.bn_stats(
                                bnbigD[:, (tt * 32 + oc2) * 6:(tt * 32 + oc2) * 6 + 6],
                                at3[:, tt, oc2 * 128:oc2 * 128 + 128])
                        ins.then_inc(s["dve"], 1)
                    pl.step("vector", v_ev)
                    dp_tevac.append(pl.inc("dve"))

            if oc >= 1:
                plan_o_tp(oc - 1)
        plan_o_tp(31)
        AO_DONE = dp_tevac[-1]


        E_F2, E_MAG = plan_ln_batch(
            AO_DONE, skip_norms=True,
            region_fn=(lambda tt: bnbigD[:, 192 * tt:192 * tt + 192]))
        E_x0ld = [E_X00]

        gw_sb = A_xb[:, 4096:4352]
        wdq_done = max(wdq_free)

        def g_ffw(g, s, wdq_done=wdq_done):
            g.wait_ge(s["pe"], wdq_done)
            g.dma_start(out=wdq[:, 0:4096].rearrange("p (kc e) -> p kc e", e=128),
                        in_=ap["acatT"].rearrange("(kc p) e -> p kc e", p=128)
                        ).then_inc(s["gld"], 16)
            g.dma_start(out=wdq[:, 4096:8192].rearrange("p (oc e) -> p oc e", e=128),
                        in_=ap["bcatT"].rearrange("p (oc e) -> p oc e", e=128)
                        ).then_inc(s["gld"], 16)
            g.dma_start(out=gw_sb.rearrange("p (kc e) -> p kc e", e=8),
                        in_=ap["gwT"].rearrange("(kc p) e -> p kc e", p=128)
                        ).then_inc(s["gld"], 16)
        pl.step("gpsimd", g_ffw)
        FF_WLD = pl.inc("gld", 48)

        ff_bld = []
        for oc_pre in range(2):
            lsn_pre = f"swl{oc_pre % 2}"

            def g_bw_pre(g, s, oc=oc_pre, lsn=lsn_pre,
                         bfree=slot_free_tp[oc_pre % 2]):
                if bfree:
                    g.wait_ge(s["pe"], bfree)
                g.dma_start(
                    out=strips[:, 4096 * (oc % 2):4096 * (oc % 2) + 4096].rearrange(
                        "p (kc o) -> p kc o", o=128),
                    in_=ap["bwT"][:, 128 * oc:128 * oc + 128].rearrange(
                        "(kc p) o -> p kc o", p=128)).then_inc(s[lsn], 16)
            pl.step("gpsimd", g_bw_pre)
            ff_bld.append((lsn_pre, pl.inc(lsn_pre, 16)))
        E_x1 = []
        E_pk = []
        E_nm2 = [0] * NT
        E_stats_f = []
        tp_hist_E = []
        E_tp_done = [0] * NT

        def plan_e_x0(tt):
            lx = f"ldx{tt % 2}"
            lp = f"ldp{tt % 2}"
            slot_ok2 = E_nm2[tt - 2] if tt >= 2 else 0

            pk_ok = E_pk[tt - 2] if tt >= 2 else None

            def sy_x0(sy_, s, tt=tt, slot_ok2=slot_ok2, pk_ok=pk_ok, lx=lx, lp=lp):
                if tt == 1:
                    sy_.wait_ge(s["dve"], E_MAG)
                if slot_ok2:
                    sy_.wait_ge(s["act"], slot_ok2)
                if pk_ok:
                    sy_.wait_ge(s[pk_ok[0]], pk_ok[1])
                sy_.dma_start(out=xslot(tt)[:, 0:2048],
                              in_=x0r[tt][:, 0:2048]).then_inc(s[lx], 16)
                sy_.dma_start(out=xslot(tt)[:, 2048:4096],
                              in_=x0r[tt][:, 2048:4096]).then_inc(s[lp], 16)
            pl.step("sync", sy_x0)
            E_x0ld.append(((lx, pl.inc(lx, 16)), (lp, pl.inc(lp, 16))))

        def plan_e_front(tt):
            def sc_n2(sc, s, tt=tt):
                if tt == 0:
                    sc.wait_ge(s["gps"], E_F2)
                sc.activation(out=at3[:, tt, :], in_=at3[:, tt, :], func=AF.Identity,
                              scale=rstd8[:, tt:tt + 1], bias=negmr8[:, tt:tt + 1]
                              ).then_inc(s["act"], 1)
            pl.step("scalar", sc_n2)
            m_n2 = pl.inc("act")

            def v_x1(v, s, tt=tt, m_n2=m_n2):
                v.wait_ge(s["act"], m_n2)
                for ch, val in E_x0ld[tt]:
                    v.wait_ge(s[ch], val)
                v.tensor_tensor(out=xslot(tt), in0=xslot(tt), in1=at3[:, tt, :],
                                op=ALU.add).then_inc(s["dve"], 1)
            pl.step("vector", v_x1)
            E_x1.append(pl.inc("dve"))

            sw = f"swl{tt % 2}"

            def g_pk(g, s, tt=tt, sw=sw):
                g.wait_ge(s["dve"], E_x1[tt])
                g.dma_start(out=x1r[tt], in_=xslot(tt)).then_inc(s[sw], 16)
            pl.step("gpsimd", g_pk)
            E_pk.append((sw, pl.inc(sw, 16)))

            bbuf = bnbuf[:, 48 * (tt % 2):48 * (tt % 2) + 48]

            def v_st(v, s, tt=tt, bbuf=bbuf):
                for i in range(8):
                    ins = v.bn_stats(bbuf[:, 6 * i:6 * i + 6],
                                     xslot(tt)[:, 512 * i:512 * i + 512])
                ins.then_inc(s["dve"], 1)
            pl.step("vector", v_st)
            m_st = pl.inc("dve")
            E_stats_f.append(plan_fence("dve", m_st))

        def plan_e_mid(tt):
            par = tt % 2
            ba = bnagg[:, 2 * par:2 * par + 2]
            rs = rstd[:, par:par + 1]
            nm = negmr[:, par:par + 1]

            def v_aggr(v, s, tt=tt, ba=ba, par=par):
                v.wait_ge(s["gps"], E_stats_f[tt])
                v.bn_aggr(ba, bnbuf[:, 48 * par:48 * par + 48]).then_inc(s["dve"], 1)
            pl.step("vector", v_aggr)
            m_aggr = pl.inc("dve")

            def sc_rstd(sc, s, ba=ba, rs=rs, m_aggr=m_aggr):
                sc.wait_ge(s["dve"], m_aggr)
                sc.activation(out=rs, in_=ba[:, 1:2], func=AF.Ln, bias=epsv[:, 0:1])
                sc.activation(out=rs, in_=rs, func=AF.Exp, scale=-0.5
                              ).then_inc(s["act"], 1)
            pl.step("scalar", sc_rstd)
            m_rstd = pl.inc("act")

            def v_negmr(v, s, ba=ba, rs=rs, nm=nm, m_rstd=m_rstd):
                v.wait_ge(s["act"], m_rstd)
                v.tensor_scalar(out=nm, in0=ba[:, 0:1], scalar1=rs, op0=ALU.mult,
                                scalar2=-1.0, op1=ALU.mult).then_inc(s["dve"], 1)
            pl.step("vector", v_negmr)
            m_negmr = pl.inc("dve")

            h2tm_free = E_tp_done[tt - 1] if tt >= 1 else 0

            def sc_n3(sc, s, tt=tt, rs=rs, nm=nm, m_negmr=m_negmr,
                      h2tm_free=h2tm_free):
                sc.wait_ge(s["dve"], m_negmr)
                if h2tm_free:
                    sc.wait_ge(s["pe"], h2tm_free)
                sc.activation(out=h1tm[:, :], in_=xslot(tt), func=AF.Identity,
                              scale=rs, bias=nm).then_inc(s["act"], 1)
            pl.step("scalar", sc_n3)
            E_nm2[tt] = pl.inc("act")

            plan_transpose_tile(
                src_fn=(lambda: h1tm),
                dst_col_fn=(lambda grp, tt=tt: A_h3[:, grp * 4:grp * 4 + 4,
                                                    tt * 128:tt * 128 + 128]),
                wait_act=E_nm2[tt], tp_banks=(0, 1), evac_hist=tp_hist_E)
            E_tp_done[tt] = pl.val("pe")

        plan_e_x0(1)
        for tt in range(NT):
            plan_e_front(tt)
            if tt >= 1:
                plan_e_mid(tt - 1)
            if 2 <= tt + 1 < NT:
                plan_e_x0(tt + 1)
        plan_e_mid(NT - 1)
        H2_DONE = tp_hist_E[-1]

        cexp_sb = A_xb[:, 4096:5120]
        lowp_sb = A_xb[:, 5120:6144]
        ffstat = lambda tt: (
            PC[:, (tt // 2) * 512 + (tt % 2) * 192:(tt // 2) * 512 + (tt % 2) * 192 + 192]
            if tt < 6 else PD[:, (tt - 6) * 192:(tt - 6) * 192 + 192])
        g_lfm = A_xb[0:8, 0:1024]
        g_cfm = A_xb[0:8, 1024:2048]

        def t_gate(t, s):
            t.wait_ge(s["dve"], H2_DONE)
            t.wait_ge(s["gld"], FF_WLD)
            gps_t = [PD[0:8, :], PC[0:8, 1024:1536]]
            for half in range(2):
                for kc in range(NG):
                    ins = t.matmul(gps_t[half], gw_sb[:, 8 * kc:8 * kc + 8],
                                   A_h3[:, kc, half * 512:half * 512 + 512],
                                   start=(kc == 0), stop=(kc == NG - 1))
            ins.then_inc(s["pe"], 1)
            for half in range(2):
                for kc in range(NG):
                    ins = t.matmul(PS[4 + half][:, :], wdq[:, 128 * kc:128 * kc + 128],
                                   A_h3[:, kc, half * 512:half * 512 + 512],
                                   start=(kc == 0), stop=(kc == NG - 1))
            ins.then_inc(s["pe"], 1)
        pl.step("tensor", t_gate)
        M_GMM = pl.inc("pe")
        M_LOWMM = pl.inc("pe")

        def sc_glog(sc, s):
            sc.wait_ge(s["pe"], M_GMM)
            sc.activation(out=g_lfm[:, 0:512], in_=PD[0:8, :], func=AF.Identity)
            sc.activation(out=g_lfm[:, 512:1024], in_=PC[0:8, 1024:1536],
                          func=AF.Identity).then_inc(s["act"], 1)
        pl.step("scalar", sc_glog)
        M_GLOG = pl.inc("act")

        def t_gtp(t, s):
            t.wait_ge(s["act"], M_GLOG)
            for tt in range(NT):
                pb = PS[2 + tt // 4]
                ins = t.matmul(pb[0:128, 8 * (tt % 4):8 * (tt % 4) + 8],
                               g_lfm[0:8, 128 * tt:128 * tt + 128], ident[0:8, 0:8],
                               start=(tt % 4 == 0), stop=True,
                               skip_group_check=(tt % 4 != 0))
            ins.then_inc(s["pe"], 1)
        pl.step("tensor", t_gtp)
        M_GTP = pl.inc("pe")

        def sc_ltm(sc, s):
            sc.wait_ge(s["pe"], M_GTP)
            sc.activation(out=g_ltm[:, 0:32], in_=PS[2][0:128, 0:32], func=AF.Identity)
            sc.activation(out=g_ltm[:, 32:64], in_=PS[3][0:128, 0:32],
                          func=AF.Identity).then_inc(s["act"], 1)
        pl.step("scalar", sc_ltm)
        M_LTM = pl.inc("act")

        def v_top_m1(v, s):
            v.wait_ge(s["act"], M_LTM)
            for tt in range(NT):
                ins = v.tensor_reduce(out=g_m1[:, tt:tt + 1], in_=g_ltm[:, 8 * tt:8 * tt + 8],
                                      op=ALU.max, axis=AX.X)
            ins.then_inc(s["dve"], 1)
        pl.step("vector", v_top_m1)
        M_G1 = pl.inc("dve")
        F_G1 = plan_fence("dve", M_G1)

        def v_top_eq1(v, s):
            v.wait_ge(s["gps"], F_G1)
            for tt in range(NT):
                v.tensor_scalar(out=g_eq1[:, 8 * tt:8 * tt + 8], in0=g_ltm[:, 8 * tt:8 * tt + 8],
                                scalar1=g_m1[:, tt:tt + 1], op0=ALU.is_ge, scalar2=None)
            for tt in range(NT):
                ins = v.scalar_tensor_tensor(out=g_msk[:, 8 * tt:8 * tt + 8],
                                             in0=g_eq1[:, 8 * tt:8 * tt + 8], scalar=-1e30,
                                             op0=ALU.mult, op1=ALU.add,
                                             in1=g_ltm[:, 8 * tt:8 * tt + 8])
            ins.then_inc(s["dve"], 1)
        pl.step("vector", v_top_eq1)
        M_G2 = pl.inc("dve")
        F_G2 = plan_fence("dve", M_G2)

        def v_top_m2(v, s):
            v.wait_ge(s["gps"], F_G2)
            for tt in range(NT):
                ins = v.tensor_reduce(out=g_m2[:, tt:tt + 1], in_=g_msk[:, 8 * tt:8 * tt + 8],
                                      op=ALU.max, axis=AX.X)
            ins.then_inc(s["dve"], 1)
        pl.step("vector", v_top_m2)
        M_G3 = pl.inc("dve")
        F_G3 = plan_fence("dve", M_G3)

        def v_top_eq2(v, s):
            v.wait_ge(s["gps"], F_G3)
            for tt in range(NT):
                ins = v.tensor_scalar(out=g_eq2[:, 8 * tt:8 * tt + 8],
                                      in0=g_msk[:, 8 * tt:8 * tt + 8],
                                      scalar1=g_m2[:, tt:tt + 1], op0=ALU.is_ge, scalar2=None)
            ins.then_inc(s["dve"], 1)
        pl.step("vector", v_top_eq2)
        M_TOPA = pl.inc("dve")

        def sc_gexp(sc, s):
            sc.wait_ge(s["dve"], M_TOPA)
            for tt in range(NT):
                ins = sc.activation(out=g_exp[:, tt:tt + 1], in_=g_m1[:, tt:tt + 1],
                                    func=AF.Exp, scale=-1.0, bias=g_m2[:, tt:tt + 1])
            ins.then_inc(s["act"], 1)
        pl.step("scalar", sc_gexp)
        M_GEXP = pl.inc("act")

        def v_top_b1(v, s):
            v.wait_ge(s["act"], M_GEXP)
            v.tensor_scalar(out=g_w2[:, :], in0=g_exp[:, :],
                            scalar1=1.0, op0=ALU.add, scalar2=None).then_inc(s["dve"], 1)
        pl.step("vector", v_top_b1)
        M_B1 = pl.inc("dve")
        F_B1 = plan_fence("dve", M_B1)

        def v_top_b2(v, s):
            v.wait_ge(s["gps"], F_B1)
            v.reciprocal(g_w1[:, :], g_w2[:, :]).then_inc(s["dve"], 1)
        pl.step("vector", v_top_b2)
        M_B2 = pl.inc("dve")
        F_B2 = plan_fence("dve", M_B2)

        def v_top_b3(v, s):
            v.wait_ge(s["gps"], F_B2)
            v.tensor_scalar(out=g_w2[:, :], in0=g_w1[:, :], scalar1=-1.0, op0=ALU.mult,
                            scalar2=1.0, op1=ALU.add).then_inc(s["dve"], 1)
        pl.step("vector", v_top_b3)
        M_TOPB = pl.inc("dve")

        def g_fence_w(g, s):
            g.wait_ge(s["dve"], M_TOPB)
            g.engine_nop().then_inc(s["gps"], 1)
        pl.step("gpsimd", g_fence_w)
        M_WF = pl.inc("gps")

        def v_top_c(v, s):
            v.wait_ge(s["gps"], M_WF)
            for tt in range(NT):
                v.tensor_scalar(out=g_ctm[:, 8 * tt:8 * tt + 8],
                                in0=g_eq1[:, 8 * tt:8 * tt + 8],
                                scalar1=g_w1[:, tt:tt + 1], op0=ALU.mult, scalar2=None)
            for tt in range(NT):
                v.scalar_tensor_tensor(out=g_ctm[:, 8 * tt:8 * tt + 8],
                                       in0=g_eq2[:, 8 * tt:8 * tt + 8],
                                       scalar=g_w2[:, tt:tt + 1], op0=ALU.mult,
                                       op1=ALU.add, in1=g_ctm[:, 8 * tt:8 * tt + 8])
            v.tensor_copy(g_ctmb[:, :], g_ctm[:, :]).then_inc(s["dve"], 1)
        pl.step("vector", v_top_c)
        M_TOP2 = pl.inc("dve")

        def t_ctp(t, s):
            t.wait_ge(s["dve"], M_TOP2)
            for tt in range(NT):
                pb = PS[2 + tt // 4]
                ins = t.matmul(pb[0:8, 128 * (tt % 4):128 * (tt % 4) + 128],
                               g_ctmb[:, 8 * tt:8 * tt + 8], ident[:, :],
                               start=(tt % 4 == 0), stop=True,
                               skip_group_check=(tt % 4 != 0))
            ins.then_inc(s["pe"], 1)
        pl.step("tensor", t_ctp)
        M_CTP = pl.inc("pe")

        def sc_cfm(sc, s):
            sc.wait_ge(s["pe"], M_CTP)
            sc.activation(out=g_cfm[:, 0:512], in_=PS[2][0:8, :], func=AF.Identity)
            sc.activation(out=g_cfm[:, 512:1024], in_=PS[3][0:8, :],
                          func=AF.Identity).then_inc(s["act"], 1)
        pl.step("scalar", sc_cfm)
        M_CFM = pl.inc("act")

        def t_cexp(t, s):
            t.wait_ge(s["act"], M_CFM)
            for half in range(2):
                ins = t.matmul(PS[2 + half][:, :], e816[:, :],
                               g_cfm[:, half * 512:half * 512 + 512], start=True, stop=True)
            ins.then_inc(s["pe"], 1)
        pl.step("tensor", t_cexp)
        M_CEXP = pl.inc("pe")

        def v_lowp(v, s):
            v.wait_ge(s["pe"], M_CEXP)
            v.tensor_copy(cexp_sb[:, 0:512], PS[2][:, :])
            v.tensor_copy(cexp_sb[:, 512:1024], PS[3][:, :])
            v.wait_ge(s["pe"], M_LOWMM)
            v.tensor_tensor(out=lowp_sb[:, 0:512], in0=PS[4][:, :], in1=cexp_sb[:, 0:512],
                            op=ALU.mult)
            v.tensor_tensor(out=lowp_sb[:, 512:1024], in0=PS[5][:, :],
                            in1=cexp_sb[:, 512:1024], op=ALU.mult).then_inc(s["dve"], 1)
        pl.step("vector", v_lowp)
        M_LOWP = pl.inc("dve")

        ff_bev = []
        ff_tp = []
        ff_tevac = []
        ff_base = []
        G_x1ld_early = {}
        for oc in range(32):
            if oc == 12:
                for tt_ in range(2):
                    gc_ = "gld" if tt_ % 2 == 0 else "gx1"

                    def g_x1e(g, s, tt_=tt_, gc_=gc_, pk_done=E_pk[tt_]):
                        g.wait_ge(s[pk_done[0]], pk_done[1])
                        g.wait_ge(s["act"], E_nm2[6 + tt_])
                        g.dma_start(out=xslot(tt_), in_=x1r[tt_]
                                    ).then_inc(s[gc_], 16)
                    pl.step("gpsimd", g_x1e)
                    G_x1ld_early[tt_] = (gc_, pl.inc(gc_, 16))
            if oc >= 2:
                bstr_free = ff_base[oc - 2]
                lsn = f"swl{oc % 2}"

                def g_bw(g, s, oc=oc, bstr_free=bstr_free, lsn=lsn):
                    if bstr_free:
                        g.wait_ge(s["pe"], bstr_free)
                    g.dma_start(
                        out=strips[:, 4096 * (oc % 2):4096 * (oc % 2) + 4096].rearrange(
                            "p (kc o) -> p kc o", o=128),
                        in_=ap["bwT"][:, 128 * oc:128 * oc + 128].rearrange(
                            "(kc p) o -> p kc o", p=128)).then_inc(s[lsn], 16)
                pl.step("gpsimd", g_bw)
                ff_bld.append((lsn, pl.inc(lsn, 16)))

            ao_free2 = ff_tp[2 * (oc - 2) + 1] if oc >= 2 else 0

            def t_base(t, s, oc=oc, ao_free2=ao_free2,
                       ev_prev=(ff_bev[oc - 1] if oc >= 1 else 0)):
                t.wait_ge(s[ff_bld[oc][0]], ff_bld[oc][1])
                if ev_prev:
                    t.wait_ge(s["act"], ev_prev)
                if oc == 0:
                    t.wait_ge(s["dve"], M_LOWP)
                sl = strips[:, 4096 * (oc % 2):4096 * (oc % 2) + 4096]
                for half in range(2):
                    for kc in range(NG):
                        t.matmul(PA[:, half * 512:half * 512 + 512],
                                 sl[:, 128 * kc:128 * kc + 128],
                                 A_h3[:, kc, half * 512:half * 512 + 512],
                                 start=(kc == 0), stop=False)
                    ins = t.matmul(PA[:, half * 512:half * 512 + 512],
                                   wdq[:, 4096 + 128 * oc:4096 + 128 * oc + 128],
                                   lowp_sb[:, half * 512:half * 512 + 512],
                                   start=False, stop=True)
                ins.then_inc(s["pe"], 1)
            pl.step("tensor", t_base)
            m_base = pl.inc("pe")
            ff_base.append(m_base)

            def sc_bev(sc, s, oc=oc, m_base=m_base,
                       ao_free=(ff_tp[2 * (oc - 2) + 1] if oc >= 2 else 0)):
                sc.wait_ge(s["pe"], m_base)
                if ao_free:
                    sc.wait_ge(s["pe"], ao_free)
                sc.activation(out=aofm(oc)[:, :], in_=PA[:, :], func=AF.Identity,
                              bias=bvec_sb[:, oc:oc + 1]).then_inc(s["act"], 1)
            pl.step("scalar", sc_bev)
            m_bev = pl.inc("act")
            ff_bev.append(m_bev)

            def plan_f_tp(oc2):
                m_bev2 = ff_bev[oc2]
                for grp in range(2):
                    pb = PS[2 + grp]
                    prev = ff_tevac[-2] if len(ff_tevac) >= 2 else 0

                    def t_tp(t, s, oc2=oc2, grp=grp, pb=pb, prev=prev,
                             m_bev2=m_bev2):
                        if grp == 0:
                            t.wait_ge(s["act"], m_bev2)
                        if prev:
                            t.wait_ge(s["dve"], prev)
                        for q in range(4):
                            ts_ = grp * 4 + q
                            ins = t.matmul(pb[:, 128 * q:128 * q + 128],
                                           aofm(oc2)[:, 128 * ts_:128 * ts_ + 128],
                                           ident[:, :], start=(q == 0), stop=True,
                                           skip_group_check=(q != 0))
                        ins.then_inc(s["pe"], 1)
                    pl.step("tensor", t_tp)
                    m_tp = pl.inc("pe")
                    ff_tp.append(m_tp)

                    def v_ev(v, s, oc2=oc2, grp=grp, pb=pb, m_tp=m_tp):
                        v.wait_ge(s["pe"], m_tp)
                        v.tensor_copy(at3[:, grp * 4:grp * 4 + 4,
                                          oc2 * 128:oc2 * 128 + 128],
                                      pb[:, :].rearrange("p (a b) -> p a b", a=4))
                        for q4 in range(4):
                            tt = grp * 4 + q4
                            ins = v.bn_stats(
                                ffstat(tt)[:, oc2 * 6:oc2 * 6 + 6],
                                at3[:, tt, oc2 * 128:oc2 * 128 + 128])
                        ins.then_inc(s["dve"], 1)
                    pl.step("vector", v_ev)
                    ff_tevac.append(pl.inc("dve"))

            if oc >= 1:
                plan_f_tp(oc - 1)
        plan_f_tp(31)
        F_DONE = ff_tevac[-1]

        G_norms, G_MAG = plan_ln_batch(F_DONE, region_fn=ffstat)
        G_x1ld = []
        G_out = []
        for tt in range(NT):
            gc = "gld" if tt % 2 == 0 else "gx1"
            lq = f"ldq{tt % 2}"
            lk = f"ldk{tt % 2}"

            if tt < 2:
                G_x1ld.append(G_x1ld_early[tt])
            else:
                def g_x1(g, s, tt=tt, gc=gc, pk_done=E_pk[tt]):
                    g.wait_ge(s[pk_done[0]], pk_done[1])
                    for ch, val in G_out[tt - 2]:
                        g.wait_ge(s[ch], val)
                    g.dma_start(out=xslot(tt), in_=x1r[tt]).then_inc(s[gc], 16)
                pl.step("gpsimd", g_x1)
                G_x1ld.append((gc, pl.inc(gc, 16)))

            m_nm = G_norms[tt]

            def v_add(v, s, tt=tt, m_nm=m_nm):
                v.wait_ge(s["act"], m_nm)
                v.wait_ge(s[G_x1ld[tt][0]], G_x1ld[tt][1])
                v.tensor_tensor(out=xslot(tt), in0=xslot(tt), in1=at3[:, tt, :],
                                op=ALU.add).then_inc(s["dve"], 1)
            pl.step("vector", v_add)
            m_add = pl.inc("dve")

            def sy_out(sy_, s, tt=tt, m_add=m_add):
                sy_.wait_ge(s["dve"], m_add)
                for j, ch in enumerate(("ldq0", "ldq1", "ldk0", "ldk1")):
                    sy_.dma_start(out=outr[tt][:, 1024 * j:1024 * j + 1024],
                                  in_=xslot(tt)[:, 1024 * j:1024 * j + 1024]
                                  ).then_inc(s[ch], 16)
            pl.step("sync", sy_out)
            G_out.append(tuple(
                (ch, pl.inc(ch, 16)) for ch in ("ldq0", "ldq1", "ldk0", "ldk1")))

        def sy_dbg(sy_, s):
            if "dbg_h1" in dbg:
                sy_.wait_ge(s["dve"], H1_DONE)
                sy_.dma_start(out=dbg["dbg_h1"][:, :], in_=A_h[:, 0:4096]
                              ).then_inc(s["dbs"], 16)
                pl.counts["dbs"] += 16
            if "dbg_contrib" in dbg:
                sy_.wait_ge(s["ldp0"], K_CONTRIB[0])
                sy_.wait_ge(s["ldp1"], K_CONTRIB[1])
                sy_.wait_ge(s["ldv0"], V_CONTRIB[0])
                sy_.wait_ge(s["ldv1"], V_CONTRIB[1])
                sy_.dma_start(out=dbg["dbg_contrib"][0:512, :], in_=cc_ap["ka_in"][:, :]
                              ).then_inc(s["dbs"], 16)
                sy_.dma_start(out=dbg["dbg_contrib"][512:1024, :], in_=cc_ap["kb_in"][:, :]
                              ).then_inc(s["dbs"], 16)
                sy_.dma_start(out=dbg["dbg_contrib"][1024:1536, :], in_=cc_ap["va_in"][:, :]
                              ).then_inc(s["dbs"], 16)
                sy_.dma_start(out=dbg["dbg_contrib"][1536:2048, :], in_=cc_ap["vb_in"][:, :]
                              ).then_inc(s["dbs"], 16)
                pl.counts["dbs"] += 64
            if "dbg_qpark" in dbg:
                sy_.wait_ge(s["ldp0"], Q_PARKED[0])
                sy_.wait_ge(s["ldp1"], Q_PARKED[1])
                sy_.dma_start(out=dbg["dbg_qpark"][:, :], in_=qpark[0:512, :]
                              ).then_inc(s["dbs"], 16)
                pl.counts["dbs"] += 16
            if "dbg_ctx" in dbg:
                sy_.wait_ge(s["dve"], CTX_DONE)
                sy_.dma_start(out=dbg["dbg_ctx"][:, :], in_=A_h[:, 0:4096]
                              ).then_inc(s["dbs"], 16)
                pl.counts["dbs"] += 16
            if "dbg_ao" in dbg:
                sy_.wait_ge(s["dve"], AO_DONE)
                sy_.dma_start(out=dbg["dbg_ao"][:, :], in_=A_big[:, 0:8192]
                              ).then_inc(s["dbs"], 16)
                pl.counts["dbs"] += 16
            if "dbg_x1" in dbg:
                sy_.wait_ge(s[E_pk[-1][0]], E_pk[-1][1])
                sy_.dma_start(out=dbg["dbg_x1"][:, :], in_=x1park[:, :]
                              ).then_inc(s["dbs"], 16)
                pl.counts["dbs"] += 16
            if "dbg_h2" in dbg:
                sy_.wait_ge(s["dve"], H2_DONE)
                sy_.dma_start(out=dbg["dbg_h2"][:, :], in_=A_h[:, 0:4096]
                              ).then_inc(s["dbs"], 16)
                pl.counts["dbs"] += 16
            if "dbg_gate" in dbg:
                sy_.wait_ge(s["dve"], M_TOP2)
                sy_.dma_start(out=dbg["dbg_gate"][:, :], in_=g_ctm[:, :]
                              ).then_inc(s["dbs"], 16)
                pl.counts["dbs"] += 16
            if "dbg_f" in dbg:
                sy_.wait_ge(s["dve"], F_DONE)
                sy_.dma_start(out=dbg["dbg_f"][:, :], in_=A_big[:, 0:8192]
                              ).then_inc(s["dbs"], 16)
                pl.counts["dbs"] += 16
        pl.step("sync", sy_dbg)

        def sy_final(sy_, s, a=None):
            for sem in ("ldx0", "ldx1", "ldp0", "ldp1",
                        "ldq0", "ldq1", "ldk0", "ldk1"):
                sy_.wait_ge(s[sem], pl.val(sem))
            if pl.val("dbs"):
                sy_.wait_ge(s["dbs"], pl.val("dbs"))
        pl.step("sync", sy_final)

        @block.sync
        def _(E):
            for f in pl.q["sync"]:
                f(E, semh)

        @block.tensor
        def _(E):
            for f in pl.q["tensor"]:
                f(E, semh)

        @block.scalar
        def _(E):
            for f in pl.q["scalar"]:
                f(E, semh)

        @block.vector
        def _(E):
            for f in pl.q["vector"]:
                f(E, semh)

        @block.gpsimd
        def _(E):
            for f in pl.q["gpsimd"]:
                f(E, semh)

    return nc


def reassemble(results):
    full = np.empty((NC * TPC, D), dtype=np.float32)
    fr = full.reshape(NC * TPC // 128, 128, D)
    for c in range(NC):
        fr[np.arange(NT) * 8 + c] = results[c]["out"].reshape(NT, 128, D)
    return full.reshape(1, NC * TPC, D)


_NC_CACHE = {}


def kernel(**inputs):
    per_core = host_prepare(inputs)
    if "nc" not in _NC_CACHE:
        _NC_CACHE["nc"] = build_nc(debug=())
    nc = _NC_CACHE["nc"]
    res = run_bass_kernel_spmd(nc, per_core, core_ids=list(range(NC)))
    return reassemble(res.results)


# version 16
# speedup vs baseline: 1.0125x; 1.0125x over previous
import math
import numpy as np
import ml_dtypes
from contextlib import ExitStack

import concourse.bass as bass
import concourse.mybir as mybir
from concourse.bass_utils import run_bass_kernel_spmd

F32 = mybir.dt.float32
BF16 = mybir.dt.bfloat16
AF = mybir.ActivationFunctionType
ALU = mybir.AluOpType
AX = mybir.AxisListType

NC = 8
TPC = 1024
NT = 8
D = 4096
NKV = 8
NQH = 32
HD = 128
NG = 32
EPS = 1e-5
SHIFT = 14.0
ISQ = 1.0 / math.sqrt(HD)

ENGS = ("sync", "tensor", "scalar", "vector", "gpsimd")


class Plan:
    def __init__(self):
        self.q = {e: [] for e in ENGS}
        self.counts = {}

    def sem_init(self, names):
        for n in names:
            self.counts[n] = 0

    def inc(self, name, n=1):
        self.counts[name] += n
        return self.counts[name]

    def val(self, name):
        return self.counts[name]

    def step(self, eng, builder):
        self.q[eng].append(builder)


def host_prepare(inputs):
    bf = ml_dtypes.bfloat16
    x = np.asarray(inputs["L0_attn_hidden_states"])

    def wq_t(qw):
        a = np.asarray(qw).astype(np.float32).astype(bf)
        K, O = a.shape
        a = a.reshape(K // 128, 128, O // 128, 128).transpose(2, 1, 0, 3)
        return np.ascontiguousarray(a.reshape(O, K))

    def sc_t(s):
        s = np.asarray(s, dtype=np.float32)
        ns = s.shape[1] // 128
        out = np.stack([s[:, si * 128:si * 128 + 128].reshape(-1)
                        for si in range(ns)])
        return np.ascontiguousarray(out)

    shared = {
        "qwT_q": wq_t(inputs["L0_attn_qw_q"]),
        "qwT_k": wq_t(inputs["L0_attn_qw_k"]),
        "qwT_v": wq_t(inputs["L0_attn_qw_v"]),
        "qwT_o": wq_t(inputs["L0_attn_qw_o"]),
        "sT_q": sc_t(inputs["L0_attn_scales_q"]),
        "sT_k": sc_t(inputs["L0_attn_scales_k"]),
        "sT_v": sc_t(inputs["L0_attn_scales_v"]),
        "sT_o": sc_t(inputs["L0_attn_scales_o"]),
        "bwT": np.ascontiguousarray(np.asarray(inputs["L0_ffn_base_w"], dtype=np.float32).T),
        "bvec": np.ascontiguousarray(
            np.asarray(inputs["L0_ffn_base_b"], dtype=np.float32).reshape(32, 128).T),
        "acatT": np.ascontiguousarray(
            np.asarray(inputs["L0_ffn_lora_A"], dtype=np.float32).transpose(2, 0, 1).reshape(D, 128)),
        "bcatT": np.ascontiguousarray(
            np.asarray(inputs["L0_ffn_lora_B"], dtype=np.float32).transpose(0, 2, 1).reshape(128, D)),
        "gwT": np.ascontiguousarray(np.asarray(inputs["L0_ffn_gate_w"], dtype=np.float32).T),
        "ident": np.eye(128, dtype=np.float32).astype(bf),
        "e816": np.ascontiguousarray(
            (2.0 * np.eye(8, dtype=np.float32)).repeat(16, axis=1)).astype(bf),
    }
    P = np.zeros((128, 128), dtype=np.float32)
    for d in range(64):
        P[d, d + 64] = -1.0
        P[64 + d, d] = 1.0
    shared["protT"] = np.ascontiguousarray(P.T).astype(bf)

    inv = 1.0 / (10000.0 ** (np.arange(0, HD, 2, dtype=np.float32) / HD))
    pos = np.arange(NC * TPC, dtype=np.float32)
    freqs = pos[:, None] * inv[None, :]
    emb = np.concatenate([freqs, freqs], axis=1)
    cos_full = np.cos(emb).astype(np.float32)
    sin_full = np.sin(emb).astype(np.float32)

    tri = np.tril(np.ones((128, 128), dtype=np.float32)).T
    per_core = []
    xr = x.reshape(NC * TPC // 128, 128, D)
    for c in range(NC):
        own = np.arange(NT) * 8 + c
        x0c = np.ascontiguousarray(xr[own].reshape(TPC, D)).astype(np.float32)
        posr = (own[:, None] * 128 + np.arange(128)[None, :]).reshape(-1)
        cosT = np.ascontiguousarray(cos_full[posr].T)
        sinT = np.ascontiguousarray(sin_full[posr].T)
        mask = np.zeros((128, 8, 128), dtype=np.float32)
        for w in range(8):
            if w < c:
                mask[:, w, :] = 1.0
            elif w == c:
                mask[:, w, :] = tri
        d = dict(shared)
        d.update(x0=x0c, cosT=cosT, sinT=sinT,
                 maskcat=np.ascontiguousarray(mask.reshape(128, 1024)).astype(bf))
        per_core.append(d)
    return per_core


def build_nc(debug=(), detect_races=False):
    nc = bass.Bass("TRN2", target_bir_lowering=False, debug=False,
                   detect_race_conditions=detect_races)
    ap = {}

    def din(name, shape, dt):
        ap[name] = nc.dram_tensor(name, shape, dt, kind="ExternalInput").ap()

    din("x0", [TPC, D], F32)
    din("qwT_q", [D, D], BF16)
    din("qwT_k", [1024, D], BF16)
    din("qwT_v", [1024, D], BF16)
    din("qwT_o", [D, D], BF16)
    din("sT_q", [32, 4096], F32)
    din("sT_k", [8, 4096], F32)
    din("sT_v", [8, 4096], F32)
    din("sT_o", [32, 4096], F32)
    din("bwT", [D, D], F32)
    din("bvec", [128, 32], F32)
    din("acatT", [D, 128], F32)
    din("bcatT", [128, D], F32)
    din("gwT", [D, 8], F32)
    din("ident", [128, 128], BF16)
    din("e816", [8, 128], BF16)
    din("protT", [128, 128], BF16)
    din("cosT", [128, TPC], F32)
    din("sinT", [128, TPC], F32)
    din("maskcat", [128, 1024], BF16)

    out_ap = nc.dram_tensor("out", [TPC, D], F32, kind="ExternalOutput").ap()
    dbg = {}
    for name, shape, dt in [
        ("dbg_h1", [128, 4096], BF16),
        ("dbg_contrib", [2048, 1024], BF16),
        ("dbg_qpark", [512, 1024], BF16),
        ("dbg_ctx", [128, 4096], BF16),
        ("dbg_ao", [128, 8192], BF16),
        ("dbg_x1", [TPC, D], BF16),
        ("dbg_h2", [128, 4096], BF16),
        ("dbg_f", [128, 8192], BF16),
        ("dbg_gate", [128, 64], F32),
    ]:
        if name in debug:
            dbg[name] = nc.dram_tensor(name, shape, dt, kind="ExternalOutput").ap()

    qpark = nc.dram_tensor("qpark", [NQH * 128, TPC], BF16).ap()
    x1park = nc.dram_tensor("x1park", [TPC, D], BF16).ap()
    cc_t = {}
    for nm in ("ka", "kb", "va", "vb"):
        cc_t[nm + "_in"] = nc.dram_tensor(f"cc{nm}_in", [512, 1024], BF16)
        cc_t[nm + "_out"] = nc.dram_tensor(f"cc{nm}_out", [8 * 512, 1024], BF16,
                                           addr_space="Shared")
    cc_ap = {k: v.ap() for k, v in cc_t.items()}

    pl = Plan()
    SEMNAMES = ["ld", "ldsc", "ldx0", "ldx1", "lds0", "lds1", "ldp0", "ldp1",
                "ldv0", "ldv1", "ldq0", "ldq1", "ldk0", "ldk1", "gld", "dbs",
                "swl0", "swl1", "gx1",
                "pe", "act", "dve", "gps", "cka", "ckb", "cva", "cvb",
                "sc", "ex", "mk", "av"]
    pl.sem_init(SEMNAMES)

    with ExitStack() as stk:
        e = stk.enter_context
        strips = e(nc.sbuf_tensor("strips", [128, 8192], BF16))
        wdq = e(nc.sbuf_tensor("wdq", [128, 8192], BF16))
        ident = e(nc.sbuf_tensor("identS", [128, 128], BF16))
        protT = e(nc.sbuf_tensor("protTS", [128, 128], BF16))
        e816 = e(nc.sbuf_tensor("e816S", [8, 128], BF16))
        ones128 = e(nc.sbuf_tensor("ones128", [128, 128], BF16))
        negshift = e(nc.sbuf_tensor("negshift", [128, 1], F32))
        epsv = e(nc.sbuf_tensor("epsv", [128, 1], F32))
        bvec_sb = e(nc.sbuf_tensor("bvec_sb", [128, 32], F32))
        bnbuf = e(nc.sbuf_tensor("bnbuf", [128, 96], F32))
        bnagg = e(nc.sbuf_tensor("bnagg", [128, 4], F32))
        varv = e(nc.sbuf_tensor("varv", [128, 2], F32))
        rstd = e(nc.sbuf_tensor("rstd", [128, 2], F32))
        negmr = e(nc.sbuf_tensor("negmr", [128, 2], F32))
        rstd8 = e(nc.sbuf_tensor("rstd8", [128, 8], F32))
        negmr8 = e(nc.sbuf_tensor("negmr8", [128, 8], F32))
        A_h = e(nc.sbuf_tensor("A_h", [128, 32768], BF16))
        A_big = e(nc.sbuf_tensor("A_big", [128, 33024], BF16))
        A_xf = e(nc.sbuf_tensor("A_xf", [128, 8192], F32))
        A_xb = e(nc.sbuf_tensor("A_xb", [128, 6144], BF16))
        g_ltm = e(nc.sbuf_tensor("g_ltm", [128, 64], F32))
        g_m1 = e(nc.sbuf_tensor("g_m1", [128, 8], F32))
        g_m2 = e(nc.sbuf_tensor("g_m2", [128, 8], F32))
        g_eq1 = e(nc.sbuf_tensor("g_eq1", [128, 64], F32))
        g_eq2 = e(nc.sbuf_tensor("g_eq2", [128, 64], F32))
        g_msk = e(nc.sbuf_tensor("g_msk", [128, 64], F32))
        g_exp = e(nc.sbuf_tensor("g_exp", [128, 8], F32))
        g_w1 = e(nc.sbuf_tensor("g_w1", [128, 8], F32))
        g_w2 = e(nc.sbuf_tensor("g_w2", [128, 8], F32))
        g_ctm = e(nc.sbuf_tensor("g_ctm", [128, 64], F32))
        g_ctmb = e(nc.sbuf_tensor("g_ctmb", [128, 64], BF16))
        scH = lambda slot: strips[32 * slot:32 * slot + 1, 0:4096]

        PA = e(nc.psum_tensor("PA", [128, 1024], F32))
        PB = e(nc.psum_tensor("PB", [128, 1024], F32))
        PC = e(nc.psum_tensor("PC", [128, 1536], F32))
        PD = e(nc.psum_tensor("PD", [128, 512], F32))
        PS = [PA[:, 0:512], PA[:, 512:1024], PB[:, 0:512], PB[:, 512:1024],
              PC[:, 0:512], PC[:, 512:1024]]
        CTXCOL = [(j // 3) * 512 + (j % 3) * 129 for j in range(8)]

        semh = {}
        for n in SEMNAMES:
            semh[n] = e(nc.semaphore(n))
        block = e(nc.Block())

        xslot = lambda i: A_xf[:, 4096 * (i % 2):4096 * (i % 2) + 4096]
        sT_regions = [(A_xf[:, 4096:5120], A_xf[:, 5120:6144]),
                      (A_xf[:, 6144:7168], A_xf[:, 7168:8192])]
        sT_cur = [0]
        bnbig = A_xf[:, 0:1536]
        bnbigD = A_xf[:, 6144:7680]
        h1tm = A_xb[:, 0:4096]
        A_h3 = A_h[:, :].rearrange("p (kc t) -> p kc t", t=1024)
        A_big3 = A_big[:, 0:32768].rearrange("p (a b) -> p a b", b=1024)
        at3 = A_big[:, 0:32768].rearrange("p (tt d) -> p tt d", d=4096)
        x0r = ap["x0"].rearrange("(t p) d -> t p d", p=128)
        x1r = x1park.rearrange("(t p) d -> t p d", p=128)
        outr = out_ap.rearrange("(t p) d -> t p d", p=128)


        def _init_gps(g, s):
            g.memset(negshift[:, :], -SHIFT)
            g.memset(epsv[:, :], EPS)
            g.memset(ones128[:, :], 1.0).then_inc(s["gps"], 1)
        pl.step("gpsimd", _init_gps)
        GPS_INIT = pl.inc("gps")

        def _init_sync(sy, s):
            sy.dma_start(out=ident[:, :], in_=ap["ident"][:, :]).then_inc(s["ld"], 16)
            sy.dma_start(out=protT[:, :], in_=ap["protT"][:, :]).then_inc(s["ld"], 16)
            sy.dma_start(out=e816[:, :], in_=ap["e816"][:, :]).then_inc(s["ld"], 16)
            sy.dma_start(out=bvec_sb[:, :], in_=ap["bvec"][:, :]).then_inc(s["ld"], 16)
        pl.step("sync", _init_sync)
        LD_INIT = pl.inc("ld", 64)

        def _init_act(sc, s):
            sc.add_instruction(mybir.InstLoadActFuncSet(
                name=nc.get_next_instruction_name(), act_func_set_id=6, ins=[], outs=[]))
        pl.step("scalar", _init_act)


        def W(eng, sem, v):
            def f(E, s):
                if v > 0:
                    E.wait_ge(s[sem], v)
            return f

        def seq(*frags):
            def f(E, s):
                for fr in frags:
                    fr(E, s)
            return f

        def plan_ln(src_tile_fn, dst_fn, pre_wait, par, extra_norm_wait=0):
            bb = bnbuf[:, 48 * par:48 * par + 48]
            ba = bnagg[:, 2 * par:2 * par + 2]
            rs = rstd[:, par:par + 1]
            nm = negmr[:, par:par + 1]
            sem0, val0 = pre_wait

            def v_stats(v, s):
                if val0 > 0:
                    v.wait_ge(s[sem0], val0)
                for i in range(8):
                    ins = v.bn_stats(bb[:, 6 * i:6 * i + 6],
                                     src_tile_fn()[:, 512 * i:512 * i + 512])
                ins.then_inc(s["dve"], 1)
            pl.step("vector", v_stats)
            m_stats = pl.inc("dve")

            def g_f(g, s):
                g.wait_ge(s["dve"], m_stats)
                g.engine_nop().then_inc(s["gps"], 1)
            pl.step("gpsimd", g_f)
            m_f = pl.inc("gps")

            def v_aggr(v, s):
                v.wait_ge(s["gps"], m_f)
                v.bn_aggr(ba[:, :], bb[:, :]).then_inc(s["dve"], 1)
            pl.step("vector", v_aggr)
            m_aggr = pl.inc("dve")

            def sc_rstd(sc, s):
                sc.wait_ge(s["dve"], m_aggr)
                sc.activation(out=rs, in_=ba[:, 1:2], func=AF.Ln, bias=epsv[:, 0:1])
                sc.activation(out=rs, in_=rs, func=AF.Exp, scale=-0.5).then_inc(s["act"], 1)
            pl.step("scalar", sc_rstd)
            m_rstd = pl.inc("act")

            def v_negmr(v, s):
                v.wait_ge(s["act"], m_rstd)
                v.tensor_scalar(out=nm, in0=ba[:, 0:1], scalar1=rs, op0=ALU.mult,
                                scalar2=-1.0, op1=ALU.mult).then_inc(s["dve"], 1)
            pl.step("vector", v_negmr)
            m_negmr = pl.inc("dve")

            wnorm = extra_norm_wait

            def sc_norm(sc, s):
                sc.wait_ge(s["dve"], m_negmr)
                if wnorm:
                    wnorm(sc, s)
                sc.activation(out=dst_fn(), in_=src_tile_fn(), func=AF.Identity,
                              scale=rs, bias=nm).then_inc(s["act"], 1)
            pl.step("scalar", sc_norm)
            return pl.inc("act")

        def plan_fence(sem="dve", val=None):
            v = pl.val(sem) if val is None else val

            def g_f(g, s, sem=sem, v=v):
                g.wait_ge(s[sem], v)
                g.engine_nop().then_inc(s["gps"], 1)
            pl.step("gpsimd", g_f)
            return pl.inc("gps")

        def plan_transpose_tile(src_fn, dst_col_fn, wait_act, tp_banks, evac_hist):
            for grp in range(8):
                pb = PS[tp_banks[grp % 2]]
                gi = len(evac_hist) + 2 * 0
                prev = evac_hist[-2] if len(evac_hist) >= 2 else 0
                wa = wait_act if grp == 0 else 0

                def t_tp(t, s, pb=pb, prev=prev, wa=wa, grp=grp, src_fn=src_fn):
                    if wa:
                        t.wait_ge(s["act"], wa)
                    if prev:
                        t.wait_ge(s["dve"], prev)
                    for q in range(4):
                        kc = grp * 4 + q
                        ins = t.matmul(pb[:, 128 * q:128 * q + 128],
                                       src_fn()[:, 128 * kc:128 * kc + 128], ident[:, :],
                                       start=(q == 0), stop=True, skip_group_check=(q != 0))
                    ins.then_inc(s["pe"], 1)
                pl.step("tensor", t_tp)
                m_tp = pl.inc("pe")

                def v_ev(v, s, pb=pb, m_tp=m_tp, grp=grp, dst_col_fn=dst_col_fn):
                    v.wait_ge(s["pe"], m_tp)
                    v.tensor_copy(dst_col_fn(grp),
                                  pb[:, :].rearrange("p (a b) -> p a b", a=4)
                                  ).then_inc(s["dve"], 1)
                pl.step("vector", v_ev)
                evac_hist.append(pl.inc("dve"))
            return evac_hist[-1]

        def plan_ln_batch(pre_dve, pre_sem="dve", width=192, norm_dst=None,
                          norm_wait=None, skip_norms=False, region_fn=None):
            agg8 = bnbuf[:, 0:16]
            if region_fn is None:
                region_fn = lambda tt: bnbig[:, tt * width:tt * width + width]

            def v_ag(v, s, pre_dve=pre_dve, pre_sem=pre_sem, region_fn=region_fn):
                v.wait_ge(s[pre_sem], pre_dve)
                for tt in range(NT):
                    ins = v.bn_aggr(agg8[:, 2 * tt:2 * tt + 2], region_fn(tt))
                ins.then_inc(s["dve"], 1)
            pl.step("vector", v_ag)
            m_ag = pl.inc("dve")
            m_f1 = plan_fence("dve", m_ag)

            def sc_rs(sc, s, m_f1=m_f1):
                sc.wait_ge(s["gps"], m_f1)
                for tt in range(NT):
                    sc.activation(out=rstd8[:, tt:tt + 1], in_=agg8[:, 2 * tt + 1:2 * tt + 2],
                                  func=AF.Ln, bias=epsv[:, 0:1])
                for tt in range(NT):
                    ins = sc.activation(out=rstd8[:, tt:tt + 1], in_=rstd8[:, tt:tt + 1],
                                        func=AF.Exp, scale=-0.5)
                ins.then_inc(s["act"], 1)
            pl.step("scalar", sc_rs)
            m_rs = pl.inc("act")

            def v_nm(v, s, m_rs=m_rs):
                v.wait_ge(s["act"], m_rs)
                for tt in range(NT):
                    ins = v.tensor_scalar(out=negmr8[:, tt:tt + 1], in0=agg8[:, 2 * tt:2 * tt + 1],
                                          scalar1=rstd8[:, tt:tt + 1], op0=ALU.mult,
                                          scalar2=-1.0, op1=ALU.mult)
                ins.then_inc(s["dve"], 1)
            pl.step("vector", v_nm)
            m_nm = pl.inc("dve")
            m_f2 = plan_fence("dve", m_nm)
            norms = []
            if skip_norms:
                return m_f2, m_ag
            for tt in range(NT):
                wfn = norm_wait(tt) if norm_wait else None

                def sc_norm(sc, s, tt=tt, m_f2=m_f2, wfn=wfn):
                    if tt == 0:
                        sc.wait_ge(s["gps"], m_f2)
                    if wfn:
                        wfn(sc, s)
                    dst = norm_dst(tt) if norm_dst else at3[:, tt, :]
                    sc.activation(out=dst, in_=at3[:, tt, :], func=AF.Identity,
                                  scale=rstd8[:, tt:tt + 1], bias=negmr8[:, tt:tt + 1]
                                  ).then_inc(s["act"], 1)
                pl.step("scalar", sc_norm)
                norms.append(pl.inc("act"))
            return norms, m_ag

        slot_free_tp = [0, 0]
        wdq_free = [0, 0]
        sch_free = [0, 0]
        deq_evac = []
        deq_pbs = [PS[4], PS[5], PC[:, 1024:1536], PD[:, 0:512]]

        def plan_dequant_strip(mat, si, slot, dst, dst_is_abig=False,
                               first_tp_wait=None, dma_wait=None):
            lsn = f"lds{slot}"
            out_w = dst if dst_is_abig else wdq[:, 4096 * slot:4096 * slot + 4096]
            free_at = 0 if dst_is_abig else wdq_free[slot]

            def sy(sy_, s, si=si, free_at=free_at, mat=mat, lsn=lsn,
                   out_w=out_w, dst_is_abig=dst_is_abig, dma_wait=dma_wait):
                if free_at:
                    sy_.wait_ge(s["pe"], free_at)
                if dma_wait:
                    sy_.wait_ge(s[dma_wait[0]], dma_wait[1])
                src = ap[f"qwT_{mat}"][128 * si:128 * si + 128, :]
                if dst_is_abig:
                    src = src.rearrange("p (kc o) -> p kc o", o=128)
                sy_.dma_start(out=out_w, in_=src).then_inc(s[lsn], 16)
            pl.step("sync", sy)
            m_ld = pl.inc(lsn, 16)

            sfree = sch_free[slot]

            def g_sc(g, s, si=si, slot=slot, mat=mat, sfree=sfree):
                if sfree:
                    g.wait_ge(s["pe"], sfree)
                g.dma_start(out=scH(slot),
                            in_=ap[f"sT_{mat}"][si:si + 1, :]
                            ).then_inc(s["ldsc"], 16)
            pl.step("gpsimd", g_sc)
            m_lsc = pl.inc("ldsc", 16)

            for grp in range(8):
                pb = deq_pbs[len(deq_evac) % 4]
                prev = deq_evac[-4] if len(deq_evac) >= 4 else 0

                def t_bc(t, s, grp=grp, pb=pb, prev=prev, m_lsc=m_lsc, slot=slot,
                         ftw=(first_tp_wait if grp == 0 else None)):
                    if grp == 0:
                        t.wait_ge(s["ldsc"], m_lsc)
                        t.wait_ge(s["gps"], GPS_INIT)
                    if ftw:
                        t.wait_ge(s[ftw[0]], ftw[1])
                    if prev:
                        t.wait_ge(s["dve"], prev)
                    ins = t.matmul(pb[:, 0:512],
                                   ones128[32 * slot:32 * slot + 1, 0:128],
                                   scH(slot)[:, 512 * grp:512 * grp + 512],
                                   start=True, stop=True)
                    ins.then_inc(s["pe"], 1)
                pl.step("tensor", t_bc)
                m_bc = pl.inc("pe")

                if dst_is_abig:
                    def v_dq(v, s, grp=grp, pb=pb, m_bc=m_bc, m_ld=m_ld, lsn=lsn,
                             dst=dst):
                        v.wait_ge(s["pe"], m_bc)
                        if grp == 0:
                            v.wait_ge(s[lsn], m_ld)
                        v.scalar_tensor_tensor(
                            out=dst[:, 4 * grp:4 * grp + 4, :],
                            in0=dst[:, 4 * grp:4 * grp + 4, :],
                            scalar=-8.0, op0=ALU.add, op1=ALU.mult,
                            in1=pb[:, 0:512].rearrange("p (a b) -> p a b", a=4)
                            ).then_inc(s["dve"], 1)
                else:
                    def v_dq(v, s, grp=grp, pb=pb, m_bc=m_bc, m_ld=m_ld, lsn=lsn,
                             slot=slot):
                        v.wait_ge(s["pe"], m_bc)
                        if grp == 0:
                            v.wait_ge(s[lsn], m_ld)
                        cl = 4096 * slot + 512 * grp
                        v.scalar_tensor_tensor(
                            out=wdq[:, cl:cl + 512], in0=wdq[:, cl:cl + 512],
                            scalar=-8.0, op0=ALU.add, op1=ALU.mult,
                            in1=pb[:, 0:512]).then_inc(s["dve"], 1)
                pl.step("vector", v_dq)
                deq_evac.append(pl.inc("dve"))
            sch_free[slot] = pl.val("pe")
            return pl.val("dve")

        A_chains = ["gld", "swl0", "swl1"]
        A_xld = []
        A_stats_f = []
        A_norm = [0] * NT
        A_tp_done = [0] * NT
        tp_hist_A = []
        A0_SPLIT = [(0, 1536), (1536, 3072), (3072, 4096)]
        for tt in range(NT):
            if tt == 0:
                def sy0(sy_, s):
                    for (a, b), ch in zip(A0_SPLIT, A_chains):
                        sy_.dma_start(out=at3[:, 0, a:b], in_=x0r[0][:, a:b]
                                      ).then_inc(s[ch], 16)
                pl.step("gpsimd", sy0)
                A_xld.append([(ch, pl.inc(ch, 16)) for ch in A_chains])
                continue
            ch = A_chains[tt % 3]

            def sy(sy_, s, tt=tt, ch=ch):
                sy_.dma_start(out=at3[:, tt, :], in_=x0r[tt]).then_inc(s[ch], 16)
            pl.step("gpsimd", sy)
            A_xld.append([(ch, pl.inc(ch, 16))])

        def plan_a_front(tt):
            bbuf = bnbuf[:, 48 * (tt % 2):48 * (tt % 2) + 48]

            def v_st(v, s, tt=tt, bbuf=bbuf):
                for ch, val in A_xld[tt]:
                    v.wait_ge(s[ch], val)
                for i in range(8):
                    ins = v.bn_stats(bbuf[:, 6 * i:6 * i + 6],
                                     at3[:, tt, 512 * i:512 * i + 512])
                ins.then_inc(s["dve"], 1)
            pl.step("vector", v_st)
            m_st = pl.inc("dve")
            A_stats_f.append(plan_fence("dve", m_st))

        def plan_a_mid(tt):
            par = tt % 2
            ba = bnagg[:, 2 * par:2 * par + 2]
            rs = rstd[:, par:par + 1]
            nm = negmr[:, par:par + 1]

            def v_aggr(v, s, tt=tt, ba=ba, par=par):
                v.wait_ge(s["gps"], A_stats_f[tt])
                v.bn_aggr(ba, bnbuf[:, 48 * par:48 * par + 48]).then_inc(s["dve"], 1)
            pl.step("vector", v_aggr)
            m_aggr = pl.inc("dve")

            def sc_rstd(sc, s, ba=ba, rs=rs, m_aggr=m_aggr):
                sc.wait_ge(s["dve"], m_aggr)
                sc.activation(out=rs, in_=ba[:, 1:2], func=AF.Ln, bias=epsv[:, 0:1])
                sc.activation(out=rs, in_=rs, func=AF.Exp, scale=-0.5
                              ).then_inc(s["act"], 1)
            pl.step("scalar", sc_rstd)
            m_rstd = pl.inc("act")

            def v_negmr(v, s, ba=ba, rs=rs, nm=nm, m_rstd=m_rstd):
                v.wait_ge(s["act"], m_rstd)
                v.tensor_scalar(out=nm, in0=ba[:, 0:1], scalar1=rs, op0=ALU.mult,
                                scalar2=-1.0, op1=ALU.mult).then_inc(s["dve"], 1)
            pl.step("vector", v_negmr)
            m_negmr = pl.inc("dve")

            h1tm_free = A_tp_done[tt - 1] if tt >= 1 else 0

            def sc_norm(sc, s, tt=tt, rs=rs, nm=nm, m_negmr=m_negmr,
                        h1tm_free=h1tm_free):
                sc.wait_ge(s["dve"], m_negmr)
                if h1tm_free:
                    sc.wait_ge(s["pe"], h1tm_free)
                sc.activation(out=h1tm, in_=at3[:, tt, :], func=AF.Identity,
                              scale=rs, bias=nm).then_inc(s["act"], 1)
            pl.step("scalar", sc_norm)
            A_norm[tt] = pl.inc("act")
            plan_transpose_tile(
                src_fn=(lambda: h1tm),
                dst_col_fn=(lambda grp, tt=tt: A_h3[:, grp * 4:grp * 4 + 4,
                                                    tt * 128:tt * 128 + 128]),
                wait_act=A_norm[tt], tp_banks=(0, 1), evac_hist=tp_hist_A)
            A_tp_done[tt] = pl.val("pe")

        for tt in range(NT):
            plan_a_front(tt)
            if tt >= 1:
                plan_a_mid(tt - 1)
        plan_a_mid(NT - 1)
        H1_DONE = tp_hist_A[-1]


        cosS = A_xf[:, 0:1024]
        sinS = A_xf[:, 1024:2048]
        tmpro = lambda i: A_xf[:, 2048 + 1024 * (i % 2):2048 + 1024 * (i % 2) + 1024]
        qbfS = lambda i: A_xb[:, 1024 * (i % 2):1024 * (i % 2) + 1024]
        qrS = lambda i: A_xb[:, 2048 + 1024 * (i % 2):2048 + 1024 * (i % 2) + 1024]

        def sy_cs(sy_, s):
            sy_.dma_start(out=cosS, in_=ap["cosT"][:, :]).then_inc(s["ld"], 16)
            sy_.dma_start(out=sinS, in_=ap["sinT"][:, :]).then_inc(s["ld"], 16)
        pl.step("sync", sy_cs)
        CS_LD = pl.inc("ld", 32)

        def plan_proj_rope(mat, n_heads, park, post_head=None):
            st_hist = []
            ev_hist = []
            rot_hist = []
            t3_hist = []
            deq_m = {0: plan_dequant_strip(mat, 0, 0, None)}
            for hh in range(n_heads):
                slot = hh % 2
                m_deq = deq_m[hh]

                def t_proj(t, s, slot=slot, m_deq=m_deq, hh=hh):
                    t.wait_ge(s["dve"], m_deq)
                    if hh == 0:
                        t.wait_ge(s["dve"], H1_DONE)
                    for half in range(2):
                        for kc in range(NG):
                            ins = t.matmul(
                                PA[:, half * 512:half * 512 + 512],
                                wdq[:, 4096 * slot + 128 * kc:4096 * slot + 128 * kc + 128],
                                A_h3[:, kc, half * 512:half * 512 + 512],
                                start=(kc == 0), stop=(kc == NG - 1))
                    ins.then_inc(s["pe"], 1)
                pl.step("tensor", t_proj)
                m_proj = pl.inc("pe")
                wdq_free[slot] = m_proj
                if hh + 1 < n_heads:
                    deq_m[hh + 1] = plan_dequant_strip(mat, hh + 1, (hh + 1) % 2, None)

                qbf_free = rot_hist[-2] if len(rot_hist) >= 2 else 0

                def sc_ev(sc, s, hh=hh, m_proj=m_proj, qbf_free=qbf_free):
                    sc.wait_ge(s["pe"], m_proj)
                    if qbf_free:
                        sc.wait_ge(s["pe"], qbf_free)
                    sc.activation(out=qbfS(hh)[:, :], in_=PA[:, :],
                                  func=AF.Identity).then_inc(s["act"], 1)
                pl.step("scalar", sc_ev)
                m_ev = pl.inc("act")
                ev_hist.append(m_ev)

                rot_free = t3_hist[-1] if t3_hist else 0

                def t_rot(t, s, hh=hh, m_ev=m_ev, rot_free=rot_free):
                    t.wait_ge(s["act"], m_ev)
                    if rot_free:
                        t.wait_ge(s["dve"], rot_free)
                    for half in range(2):
                        ins = t.matmul(PB[:, half * 512:half * 512 + 512], protT[:, :],
                                       qbfS(hh)[:, half * 512:half * 512 + 512],
                                       start=True, stop=True)
                    ins.then_inc(s["pe"], 1)
                pl.step("tensor", t_rot)
                m_rot = pl.inc("pe")
                rot_hist.append(m_rot)

                tmp_free = t3_hist[-2] if len(t3_hist) >= 2 else 0

                def g_t1(g, s, hh=hh, m_ev=m_ev, tmp_free=tmp_free):
                    g.wait_ge(s["act"], m_ev)
                    if hh == 0:
                        g.wait_ge(s["ld"], CS_LD)
                    if tmp_free:
                        g.wait_ge(s["dve"], tmp_free)
                    g.tensor_tensor(out=tmpro(hh), in0=qbfS(hh)[:, :], in1=cosS,
                                    op=ALU.mult).then_inc(s["gps"], 1)
                pl.step("gpsimd", g_t1)
                m_t1 = pl.inc("gps")

                qr_free = st_hist[-2] if len(st_hist) >= 2 else None

                def v_ropes(v, s, hh=hh, m_rot=m_rot, m_t1=m_t1, qr_free=qr_free):
                    v.wait_ge(s["pe"], m_rot)
                    v.tensor_tensor(out=PB[:, :], in0=PB[:, :], in1=sinS[:, :],
                                    op=ALU.mult)
                    v.wait_ge(s["gps"], m_t1)
                    if qr_free:
                        v.wait_ge(s[qr_free[0]], qr_free[1])
                    v.tensor_tensor(out=qrS(hh)[:, :], in0=PB[:, :],
                                    in1=tmpro(hh)[:, :], op=ALU.add
                                    ).then_inc(s["dve"], 1)
                pl.step("vector", v_ropes)
                m_t3 = pl.inc("dve")
                t3_hist.append(m_t3)

                lpn = f"ldp{hh % 2}"

                def sy_st(sy_, s, hh=hh, m_t3=m_t3, park=park, lpn=lpn):
                    sy_.wait_ge(s["dve"], m_t3)
                    if park:
                        dst = qpark[hh * 128:hh * 128 + 128, :]
                    else:
                        half = "ka" if hh < 4 else "kb"
                        dst = cc_ap[half + "_in"][(hh % 4) * 128:(hh % 4) * 128 + 128, :]
                    sy_.dma_start(out=dst, in_=qrS(hh)[:, :]).then_inc(s[lpn], 16)
                pl.step("sync", sy_st)
                st_hist.append((lpn, pl.inc(lpn, 16)))
                if post_head is not None:
                    post_head(hh)
            return (pl.val("ldp0"), pl.val("ldp1"))

        Q_PARKED = plan_proj_rope("q", NQH, park=True)

        wv3 = A_big[:, 0:32768].rearrange("p (kc sd) -> p kc sd", sd=1024)
        for si in range(8):
            plan_dequant_strip(
                "v", si, si % 2, dst=wv3[:, :, si * 128:si * 128 + 128],
                dst_is_abig=True, dma_wait=("act", A_norm[NT - 1]))
        WV_DONE = pl.val("dve")
        vbf = lambda i: A_xb[:, 4096 + 1024 * (i % 2):4096 + 1024 * (i % 2) + 1024]
        v_ev_hist = []
        v_st_hist = []
        for ts_ in range(NT):
            wev_prev = v_ev_hist[-1] if v_ev_hist else 0

            def t_vp(t, s, ts_=ts_, wev_prev=wev_prev):
                if ts_ == 0:
                    t.wait_ge(s["dve"], WV_DONE)
                    t.wait_ge(s["dve"], H1_DONE)
                if ts_ >= 1:
                    t.wait_ge(s["act"], wev_prev)
                for half in range(2):
                    for kc in range(NG):
                        ins = t.matmul(PA[:, half * 512:half * 512 + 512],
                                       A_h3[:, kc, ts_ * 128:ts_ * 128 + 128],
                                       wv3[:, kc, half * 512:half * 512 + 512],
                                       start=(kc == 0), stop=(kc == NG - 1))
                ins.then_inc(s["pe"], 1)
            pl.step("tensor", t_vp)
            m_vp = pl.inc("pe")

            vst_free = v_st_hist[-2] if len(v_st_hist) >= 2 else None

            def sc_vev(sc, s, ts_=ts_, m_vp=m_vp, vst_free=vst_free):
                sc.wait_ge(s["pe"], m_vp)
                if vst_free:
                    sc.wait_ge(s[vst_free[0]], vst_free[1])
                sc.activation(out=vbf(ts_)[:, :], in_=PA[:, :],
                              func=AF.Identity).then_inc(s["act"], 1)
            pl.step("scalar", sc_vev)
            m_vev = pl.inc("act")
            v_ev_hist.append(m_vev)

            lvn = f"ldv{ts_ % 2}"

            def sy_vst(sy_, s, ts_=ts_, m_vev=m_vev, lvn=lvn):
                sy_.wait_ge(s["act"], m_vev)
                half = "va" if ts_ < 4 else "vb"
                sy_.dma_start(out=cc_ap[half + "_in"][(ts_ % 4) * 128:(ts_ % 4) * 128 + 128, :],
                              in_=vbf(ts_)[:, :]).then_inc(s[lvn], 16)
            pl.step("sync", sy_vst)
            v_st_hist.append((lvn, pl.inc(lvn, 16)))
            if ts_ == 3:
                VA0, VA1 = pl.val("ldv0"), pl.val("ldv1")

                def g_ccva(g, s, VA0=VA0, VA1=VA1):
                    g.wait_ge(s["ldv0"], VA0)
                    g.wait_ge(s["ldv1"], VA1)
                    g.collective_compute(
                        "AllGather", ALU.bypass, replica_groups=[list(range(NC))],
                        ins=[cc_t["va_in"].ap().opt()], outs=[cc_t["va_out"].ap().opt()]
                    ).then_inc(s["cva"], 1)
                pl.step("gpsimd", g_ccva)
                CCVA_DONE = pl.inc("cva")
            if ts_ == 7:
                VB0, VB1 = pl.val("ldv0"), pl.val("ldv1")

                def g_ccvb(g, s, VB0=VB0, VB1=VB1):
                    g.wait_ge(s["ldv0"], VB0)
                    g.wait_ge(s["ldv1"], VB1)
                    g.collective_compute(
                        "AllGather", ALU.bypass, replica_groups=[list(range(NC))],
                        ins=[cc_t["vb_in"].ap().opt()], outs=[cc_t["vb_out"].ap().opt()]
                    ).then_inc(s["cvb"], 1)
                pl.step("gpsimd", g_ccvb)
                CCVB_DONE = pl.inc("cvb")
        V_CONTRIB = (pl.val("ldv0"), pl.val("ldv1"))
        V_PROJ_DONE = pl.val("pe")


        CCK_M = {}

        def k_hook(hh):
            if hh == 3 or hh == 7:
                half = "ka" if hh == 3 else "kb"
                semn = "cka" if hh == 3 else "ckb"
                P0, P1 = pl.val("ldp0"), pl.val("ldp1")

                def g_cck(g, s, P0=P0, P1=P1, half=half, semn=semn):
                    g.wait_ge(s["ldp0"], P0)
                    g.wait_ge(s["ldp1"], P1)
                    g.collective_compute(
                        "AllGather", ALU.bypass, replica_groups=[list(range(NC))],
                        ins=[cc_t[half + "_in"].ap().opt()],
                        outs=[cc_t[half + "_out"].ap().opt()]).then_inc(s[semn], 1)
                pl.step("gpsimd", g_cck)
                CCK_M[half] = pl.inc(semn)

        K_CONTRIB = plan_proj_rope("k", NKV, park=False, post_head=k_hook)

        def g_ones(g, s):
            g.wait_ge(s["pe"], V_PROJ_DONE)
            for p in range(2):
                vv = A_big[:, 16384 + 8256 * p:16384 + 8256 * p + 8256]
                g.memset(vv.rearrange("p (s c) -> p s c", c=129)[:, :, 128:129], 1.0)
            g.engine_nop().then_inc(s["gps"], 1)
        pl.step("gpsimd", g_ones)
        ONES_DONE = pl.inc("gps")

        Kbuf = lambda p: A_big[:, 8192 * p:8192 * p + 8192]
        Vbuf = lambda p: A_big[:, 16384 + 8256 * p:16384 + 8256 * p + 8256]
        probsS = lambda p: A_xb[:, 1024 * p:1024 * p + 1024]
        qbufS2 = lambda p: A_xb[:, 2048 + 1024 * p:2048 + 1024 * p + 1024]
        maskS = A_xb[:, 4096:5120]
        ctxtm = A_xb[:, 5120:6144]
        rden_v = A_xf[:, 0:8]
        grKA = cc_ap["ka_out"].rearrange("(r x) c -> r x c", x=512)
        grKB = cc_ap["kb_out"].rearrange("(r x) c -> r x c", x=512)
        grVA = cc_ap["va_out"].rearrange("(r x) c -> r x c", x=512)
        grVB = cc_ap["vb_out"].rearrange("(r x) c -> r x c", x=512)

        def sy_mask(sy_, s):
            sy_.wait_ge(s["cka"], CCK_M["ka"])
            sy_.dma_start(out=maskS, in_=ap["maskcat"][:, :]).then_inc(s["ld"], 16)
        pl.step("sync", sy_mask)
        MASK_LD = pl.inc("ld", 16)

        batches = []
        for m in range(4):
            for bb in range(4 * m + 4):
                shared = bb <= 4 * m + 1
                kts = [4 * bb + i for i in range(4)]
                msk = None
                if bb >= 4 * m:
                    msk = "A" if (bb - 4 * m) % 2 == 0 else "B"
                batches.append((m, bb, shared, kts, msk))
        NB = len(batches)

        kvK_ld = []
        kvVA_ld = []
        kvVB_ld = []
        CTPEV = []
        head_av = []
        pend_tp = []
        last_ctp = [0]

        def plan_pending_tp():
            while pend_tp:
                qi0, m_ctxev0 = pend_tp.pop(0)
                for grp in range(2):
                    pd_free = CTPEV[-1] if CTPEV else 0

                    def t_ctp(t, s, qi=qi0, grp=grp, m_ctxev=m_ctxev0, pd_free=pd_free):
                        if grp == 0:
                            t.wait_ge(s["dve"], m_ctxev)
                        if pd_free:
                            t.wait_ge(s["dve"], pd_free)
                        for q4 in range(4):
                            j = grp * 4 + q4
                            ins = t.matmul(PD[:, 128 * q4:128 * q4 + 128],
                                           ctxtm[:, 128 * j:128 * j + 128], ident[:, :],
                                           start=(q4 == 0), stop=True,
                                           skip_group_check=(q4 != 0))
                        ins.then_inc(s["pe"], 1)
                    pl.step("tensor", t_ctp)
                    m_ctp = pl.inc("pe")
                    last_ctp[0] = m_ctp

                    def v_ctpev(v, s, qi=qi0, grp=grp, m_ctp=m_ctp):
                        v.wait_ge(s["pe"], m_ctp)
                        v.tensor_copy(
                            A_h[:, 1024 * qi + 512 * grp:1024 * qi + 512 * grp + 512],
                            PD[:, :]).then_inc(s["dve"], 1)
                    pl.step("vector", v_ctpev)
                    CTPEV.append(pl.inc("dve"))

        qh_sc = []
        evac_m = []
        exp_m = []
        mask_m = {}
        sc_m = []

        for h in range(NKV):
            p = h % 2
            kv_free = head_av[h - 2] if h >= 2 else 0
            lkn = f"ldk{p}"
            grk = grKA if h < 4 else grKB
            cksem = "cka" if h < 4 else "ckb"
            ckm = ("ka" if h < 4 else "kb")

            def sy_k(sy_, s, h=h, p=p, kv_free=kv_free, lkn=lkn, grk=grk,
                     cksem=cksem, ckm=ckm):
                sy_.wait_ge(s[cksem], CCK_M[ckm])
                if kv_free:
                    sy_.wait_ge(s["av"], kv_free)
                for r in range(NC):
                    sy_.dma_start(out=Kbuf(p)[:, 1024 * r:1024 * r + 1024],
                                  in_=grk[r, (h % 4) * 128:(h % 4) * 128 + 128, :]
                                  ).then_inc(s[lkn], 16)
            pl.step("sync", sy_k)
            kvK_ld.append((lkn, pl.inc(lkn, 16 * 8)))

            def sy_va(sy_, s, h=h, p=p, lkn=lkn):
                sy_.wait_ge(s["cva"], CCVA_DONE)
                for r in range(NC):
                    srcv = grVA[r, :, h * 128:h * 128 + 128]
                    srcv = srcv.rearrange("(lt ki) dv -> ki lt dv", ki=128)
                    dstv = Vbuf(p)[:, 1032 * r:1032 * r + 516].rearrange(
                        "p (lt c) -> p lt c", c=129)[:, :, 0:128]
                    sy_.dma_start(out=dstv, in_=srcv).then_inc(s[lkn], 16)
            pl.step("sync", sy_va)
            kvVA_ld.append((lkn, pl.inc(lkn, 16 * 8)))

            def sy_vb(sy_, s, h=h, p=p, lkn=lkn):
                sy_.wait_ge(s["cvb"], CCVB_DONE)
                for r in range(NC):
                    srcv = grVB[r, :, h * 128:h * 128 + 128]
                    srcv = srcv.rearrange("(lt ki) dv -> ki lt dv", ki=128)
                    dstv = Vbuf(p)[:, 1032 * r + 516:1032 * r + 1032].rearrange(
                        "p (lt c) -> p lt c", c=129)[:, :, 0:128]
                    sy_.dma_start(out=dstv, in_=srcv).then_inc(s[lkn], 16)
            pl.step("sync", sy_vb)
            kvVB_ld.append((lkn, pl.inc(lkn, 16 * 8)))

            for qh in range(4):
                qi = h * 4 + qh
                qb_free = qh_sc[qi - 2] if qi >= 2 else 0
                lqn = f"ldq{qi % 2}"

                def sy_q(sy_, s, qi=qi, qb_free=qb_free, lqn=lqn):
                    if qb_free:
                        sy_.wait_ge(s["sc"], qb_free)
                    sy_.dma_start(out=qbufS2(qi % 2), in_=qpark[qi * 128:qi * 128 + 128, :]
                                  ).then_inc(s[lqn], 16)
                pl.step("sync", sy_q)
                q_ld = (lqn, pl.inc(lqn, 16))

                ctx_free = evac_m[qi - 1] if qi >= 1 else 0
                vb_waited = [False]
                av_mv = {}

                def plan_av(n):
                    m, bb, shared, kts, msk = batches[n]
                    gi = qi * NB + n
                    need_b = kts[-1] >= 32 and not vb_waited[0]
                    if need_b:
                        vb_waited[0] = True

                    def t_av(t, s, h=h, p=p, qi=qi, n=n, m=m, bb=bb, shared=shared,
                             kts=tuple(kts), msk=msk, gi=gi, ctx_free=ctx_free,
                             need_b=need_b):
                        if n == 0:
                            t.wait_ge(s[kvVA_ld[h][0]], kvVA_ld[h][1])
                            if ctx_free:
                                t.wait_ge(s["dve"], ctx_free)
                            if qi == 0:
                                t.wait_ge(s["gps"], ONES_DONE)
                        if need_b:
                            t.wait_ge(s[kvVB_ld[h][0]], kvVB_ld[h][1])
                        pbuf = probsS(n % 2)
                        j0, j1 = 2 * m, 2 * m + 1
                        if shared and msk is not None and not (m == 0 and bb == 0):
                            t.wait_ge(s["ex"], exp_m[gi])
                            for kk in range(4):
                                kt = kts[kk]
                                slot = (kt % 8) * 8 + kt // 8
                                stt = (bb == 0 and kk == 0 and j1 == 3)
                                t.matmul(PC[:, CTXCOL[j1]:CTXCOL[j1] + 129],
                                         pbuf[:, 256 * kk + 128:256 * kk + 256],
                                         Vbuf(p)[:, 129 * slot:129 * slot + 129],
                                         start=stt, stop=True, skip_group_check=not stt)
                            t.wait_ge(s["mk"], mask_m[gi])
                            for kk in range(4):
                                kt = kts[kk]
                                slot = (kt % 8) * 8 + kt // 8
                                stt = (bb == 0 and kk == 0 and (j0 == 0 or (j0 == 6 and j0 == 2 * m)))
                                ins = t.matmul(PC[:, CTXCOL[j0]:CTXCOL[j0] + 129],
                                               pbuf[:, 256 * kk:256 * kk + 128],
                                               Vbuf(p)[:, 129 * slot:129 * slot + 129],
                                               start=stt, stop=True,
                                               skip_group_check=not stt)
                        else:
                            if msk is not None:
                                t.wait_ge(s["mk"], mask_m[gi])
                            else:
                                t.wait_ge(s["ex"], exp_m[gi])
                            for kk in range(4):
                                kt = kts[kk]
                                slot = (kt % 8) * 8 + kt // 8
                                if shared:
                                    pairs = [(j0, 256 * kk), (j1, 256 * kk + 128)]
                                else:
                                    pairs = [(j1, 128 * kk)]
                                for (j, pcol) in pairs:
                                    stt = (bb == 0 and kk == 0 and
                                           ((j == 0) or (j == 3) or (j == 6 and j == 2 * m)))
                                    ins = t.matmul(
                                        PC[:, CTXCOL[j]:CTXCOL[j] + 129],
                                        pbuf[:, pcol:pcol + 128],
                                        Vbuf(p)[:, 129 * slot:129 * slot + 129],
                                        start=stt, stop=True,
                                        skip_group_check=not stt)
                        ins.then_inc(s["av"], 1)
                    pl.step("tensor", t_av)
                    av_mv[n] = pl.inc("av")

                def plan_sc(n):
                    m, bb, shared, kts, msk = batches[n]
                    gi = qi * NB + n
                    S = PA if n % 2 == 0 else PB
                    wexp = exp_m[gi - 2] if gi >= 2 else 0

                    def t_sc(t, s, h=h, p=p, qi=qi, n=n, m=m, bb=bb, shared=shared,
                             kts=tuple(kts), S=S, wexp=wexp, q_ld=q_ld):
                        if n == 0:
                            t.wait_ge(s[kvK_ld[h][0]], kvK_ld[h][1])
                            t.wait_ge(s[q_ld[0]], q_ld[1])
                            if qi == 0:
                                t.wait_ge(s["ld"], MASK_LD)
                        if wexp:
                            t.wait_ge(s["ex"], wexp)
                        for kk in range(4):
                            kt = kts[kk]
                            slot = (kt % 8) * 8 + kt // 8
                            if shared:
                                out = S[:, 256 * kk:256 * kk + 256]
                                rhs = qbufS2(qi % 2)[:, 256 * m:256 * m + 256]
                                stt = (kk % 2 == 0)
                            else:
                                out = S[:, 128 * kk:128 * kk + 128]
                                rhs = qbufS2(qi % 2)[:, 128 * (2 * m + 1):
                                                     128 * (2 * m + 1) + 128]
                                stt = (kk == 0)
                            ins = t.matmul(out, Kbuf(p)[:, 128 * slot:128 * slot + 128],
                                           rhs, start=stt, stop=True,
                                           skip_group_check=not stt)
                        ins.then_inc(s["sc"], 1)
                    pl.step("tensor", t_sc)
                    sc_m.append(pl.inc("sc"))

                def plan_exp(n, wait_av=None):
                    m, bb, shared, kts, msk = batches[n]
                    gi = qi * NB + n
                    S = PA if n % 2 == 0 else PB

                    wav = av_mv[wait_av] if wait_av is not None else None

                    def sc_exp(sc, s, gi=gi, n=n, S=S, shared=shared, wav=wav):
                        sc.wait_ge(s["sc"], sc_m[gi])
                        if wav is not None:
                            sc.wait_ge(s["av"], wav)
                        w = 1024 if shared else 512
                        sc.activation(out=probsS(n % 2)[:, 0:w], in_=S[:, 0:w],
                                      func=AF.Exp, scale=ISQ, bias=negshift[:, 0:1]
                                      ).then_inc(s["ex"], 1)
                    pl.step("scalar", sc_exp)
                    exp_m.append(pl.inc("ex"))

                    if msk is not None:
                        def v_mask(v, s, gi=gi, n=n, shared=shared, msk=msk):
                            v.wait_ge(s["ex"], exp_m[gi])
                            mk_half = maskS[:, 0:512] if msk == "A" else maskS[:, 512:1024]
                            pbuf = probsS(n % 2)
                            if shared:
                                v.tensor_tensor(
                                    out=pbuf.rearrange("p (kk two q) -> p kk two q",
                                                       two=2, q=128)[:, :, 0, :],
                                    in0=pbuf.rearrange("p (kk two q) -> p kk two q",
                                                       two=2, q=128)[:, :, 0, :],
                                    in1=mk_half.rearrange("p (kk q) -> p kk q", q=128),
                                    op=ALU.mult).then_inc(s["mk"], 1)
                            else:
                                v.tensor_tensor(out=pbuf[:, 0:512], in0=pbuf[:, 0:512],
                                                in1=mk_half, op=ALU.mult
                                                ).then_inc(s["mk"], 1)
                        pl.step("vector", v_mask)
                        mask_m[gi] = pl.inc("mk")

                for n in range(0, NB, 2):
                    plan_sc(n)
                    plan_exp(n)
                    plan_sc(n + 1)
                    if n >= 2:
                        plan_av(n - 1)
                    plan_exp(n + 1, wait_av=(n - 1 if n >= 2 else None))
                    plan_av(n)
                    if n == 2:
                        plan_pending_tp()
                plan_av(NB - 1)

                qh_sc.append(pl.val("sc"))
                m_av_qh = pl.val("av")

                def v_recip(v, s, m_av_qh=m_av_qh):
                    v.wait_ge(s["av"], m_av_qh)
                    for j in range(8):
                        ins = v.reciprocal(rden_v[:, j:j + 1],
                                           PC[:, CTXCOL[j] + 128:CTXCOL[j] + 129])
                    ins.then_inc(s["dve"], 1)
                pl.step("vector", v_recip)
                m_recip = pl.inc("dve")
                m_rf = plan_fence("dve", m_recip)
                ctm_free = last_ctp[0]

                def v_ctxev(v, s, m_rf=m_rf, ctm_free=ctm_free):
                    v.wait_ge(s["gps"], m_rf)
                    if ctm_free:
                        v.wait_ge(s["pe"], ctm_free)
                    for j in range(8):
                        ins = v.tensor_scalar(
                            out=ctxtm[:, 128 * j:128 * j + 128],
                            in0=PC[:, CTXCOL[j]:CTXCOL[j] + 128],
                            scalar1=rden_v[:, j:j + 1], op0=ALU.mult, scalar2=None)
                    ins.then_inc(s["dve"], 1)
                pl.step("vector", v_ctxev)
                m_ctxev = pl.inc("dve")
                pend_tp.append((qi, m_ctxev))
                evac_m.append(m_ctxev)
            head_av.append(pl.val("av"))
        plan_pending_tp()
        CTX_DONE = CTPEV[-1]

        aofm = lambda i: A_xb[:, 1024 * (i % 2):1024 * (i % 2) + 1024]
        dp_ev = []
        dp_tp = []
        dp_tevac = []
        deq_mo = {0: plan_dequant_strip("o", 0, 0, None,
                                        first_tp_wait=("dve", CTX_DONE))}
        for oc in range(32):
            m_deq = deq_mo[oc]

            def t_op(t, s, oc=oc, m_deq=m_deq,
                     ev_prev=(dp_ev[oc - 1] if oc >= 1 else 0)):
                t.wait_ge(s["dve"], m_deq)
                if ev_prev:
                    t.wait_ge(s["act"], ev_prev)
                if oc == 0:
                    t.wait_ge(s["dve"], CTX_DONE)
                for half in range(2):
                    for cc in range(NG):
                        ins = t.matmul(
                            PA[:, half * 512:half * 512 + 512],
                            wdq[:, 4096 * (oc % 2) + 128 * cc:4096 * (oc % 2) + 128 * cc + 128],
                            A_h3[:, cc, half * 512:half * 512 + 512],
                            start=(cc == 0), stop=(cc == NG - 1))
                ins.then_inc(s["pe"], 1)
            pl.step("tensor", t_op)
            m_op = pl.inc("pe")
            wdq_free[oc % 2] = m_op
            if oc + 1 < 32:
                deq_mo[oc + 1] = plan_dequant_strip("o", oc + 1, (oc + 1) % 2, None)
            if oc == 2:
                def sy_x00(sy_, s):
                    sy_.wait_ge(s["dve"], CTX_DONE)
                    sy_.dma_start(out=xslot(0)[:, 0:2048],
                                  in_=x0r[0][:, 0:2048]).then_inc(s["ldx0"], 16)
                    sy_.dma_start(out=xslot(0)[:, 2048:4096],
                                  in_=x0r[0][:, 2048:4096]).then_inc(s["ldp0"], 16)
                pl.step("sync", sy_x00)
                E_X00 = (("ldx0", pl.inc("ldx0", 16)), ("ldp0", pl.inc("ldp0", 16)))

            def sc_oev(sc, s, oc=oc, m_op=m_op, ao_free=(dp_tp[2 * (oc - 2) + 1] if oc >= 2 else 0)):
                sc.wait_ge(s["pe"], m_op)
                if ao_free:
                    sc.wait_ge(s["pe"], ao_free)
                sc.activation(out=aofm(oc)[:, 0:512], in_=PS[0][:, :], func=AF.Identity)
                sc.activation(out=aofm(oc)[:, 512:1024], in_=PS[1][:, :],
                              func=AF.Identity).then_inc(s["act"], 1)
            pl.step("scalar", sc_oev)
            m_oev = pl.inc("act")
            dp_ev.append(m_oev)

            def plan_o_tp(oc2):
                m_oev2 = dp_ev[oc2]
                for grp in range(2):
                    pb = PS[2 + grp]
                    prev = dp_tevac[-2] if len(dp_tevac) >= 2 else 0

                    def t_tp(t, s, oc2=oc2, grp=grp, pb=pb, prev=prev,
                             m_oev2=m_oev2):
                        if grp == 0:
                            t.wait_ge(s["act"], m_oev2)
                        if prev:
                            t.wait_ge(s["dve"], prev)
                        for q in range(4):
                            ts_ = grp * 4 + q
                            ins = t.matmul(pb[:, 128 * q:128 * q + 128],
                                           aofm(oc2)[:, 128 * ts_:128 * ts_ + 128],
                                           ident[:, :], start=(q == 0), stop=True,
                                           skip_group_check=(q != 0))
                        ins.then_inc(s["pe"], 1)
                    pl.step("tensor", t_tp)
                    m_tp = pl.inc("pe")
                    dp_tp.append(m_tp)

                    def v_ev(v, s, oc2=oc2, grp=grp, pb=pb, m_tp=m_tp):
                        v.wait_ge(s["pe"], m_tp)
                        v.tensor_copy(at3[:, grp * 4:grp * 4 + 4,
                                          oc2 * 128:oc2 * 128 + 128],
                                      pb[:, :].rearrange("p (a b) -> p a b", a=4))
                        for q4 in range(4):
                            tt = grp * 4 + q4
                            ins = v.bn_stats(
                                bnbigD[:, (tt * 32 + oc2) * 6:(tt * 32 + oc2) * 6 + 6],
                                at3[:, tt, oc2 * 128:oc2 * 128 + 128])
                        ins.then_inc(s["dve"], 1)
                    pl.step("vector", v_ev)
                    dp_tevac.append(pl.inc("dve"))

            if oc >= 1:
                plan_o_tp(oc - 1)
        plan_o_tp(31)
        AO_DONE = dp_tevac[-1]


        E_F2, E_MAG = plan_ln_batch(
            AO_DONE, skip_norms=True,
            region_fn=(lambda tt: bnbigD[:, 192 * tt:192 * tt + 192]))
        E_x0ld = [E_X00]

        gw_sb = A_xb[:, 4096:4352]
        wdq_done = max(wdq_free)

        def g_ffw(g, s, wdq_done=wdq_done):
            g.wait_ge(s["pe"], wdq_done)
            g.dma_start(out=wdq[:, 0:4096].rearrange("p (kc e) -> p kc e", e=128),
                        in_=ap["acatT"].rearrange("(kc p) e -> p kc e", p=128)
                        ).then_inc(s["gld"], 16)
            g.dma_start(out=wdq[:, 4096:8192].rearrange("p (oc e) -> p oc e", e=128),
                        in_=ap["bcatT"].rearrange("p (oc e) -> p oc e", e=128)
                        ).then_inc(s["gld"], 16)
            g.dma_start(out=gw_sb.rearrange("p (kc e) -> p kc e", e=8),
                        in_=ap["gwT"].rearrange("(kc p) e -> p kc e", p=128)
                        ).then_inc(s["gld"], 16)
        pl.step("gpsimd", g_ffw)
        FF_WLD = pl.inc("gld", 48)

        ff_bld = []
        for oc_pre in range(2):
            lsn_pre = f"swl{oc_pre % 2}"

            def g_bw_pre(g, s, oc=oc_pre, lsn=lsn_pre,
                         bfree=slot_free_tp[oc_pre % 2]):
                if bfree:
                    g.wait_ge(s["pe"], bfree)
                g.dma_start(
                    out=strips[:, 4096 * (oc % 2):4096 * (oc % 2) + 4096].rearrange(
                        "p (kc o) -> p kc o", o=128),
                    in_=ap["bwT"][:, 128 * oc:128 * oc + 128].rearrange(
                        "(kc p) o -> p kc o", p=128)).then_inc(s[lsn], 16)
            pl.step("gpsimd", g_bw_pre)
            ff_bld.append((lsn_pre, pl.inc(lsn_pre, 16)))
        E_x1 = []
        E_pk = []
        E_nm2 = [0] * NT
        E_stats_f = []
        tp_hist_E = []
        E_tp_done = [0] * NT

        def plan_e_x0(tt):
            lx = f"ldx{tt % 2}"
            lp = f"ldp{tt % 2}"
            slot_ok2 = E_nm2[tt - 2] if tt >= 2 else 0

            pk_ok = E_pk[tt - 2] if tt >= 2 else None

            def sy_x0(sy_, s, tt=tt, slot_ok2=slot_ok2, pk_ok=pk_ok, lx=lx, lp=lp):
                if tt == 1:
                    sy_.wait_ge(s["dve"], E_MAG)
                if slot_ok2:
                    sy_.wait_ge(s["act"], slot_ok2)
                if pk_ok:
                    sy_.wait_ge(s[pk_ok[0]], pk_ok[1])
                sy_.dma_start(out=xslot(tt)[:, 0:2048],
                              in_=x0r[tt][:, 0:2048]).then_inc(s[lx], 16)
                sy_.dma_start(out=xslot(tt)[:, 2048:4096],
                              in_=x0r[tt][:, 2048:4096]).then_inc(s[lp], 16)
            pl.step("sync", sy_x0)
            E_x0ld.append(((lx, pl.inc(lx, 16)), (lp, pl.inc(lp, 16))))

        def plan_e_front(tt):
            def sc_n2(sc, s, tt=tt):
                if tt == 0:
                    sc.wait_ge(s["gps"], E_F2)
                sc.activation(out=at3[:, tt, :], in_=at3[:, tt, :], func=AF.Identity,
                              scale=rstd8[:, tt:tt + 1], bias=negmr8[:, tt:tt + 1]
                              ).then_inc(s["act"], 1)
            pl.step("scalar", sc_n2)
            m_n2 = pl.inc("act")

            def v_x1(v, s, tt=tt, m_n2=m_n2):
                v.wait_ge(s["act"], m_n2)
                for ch, val in E_x0ld[tt]:
                    v.wait_ge(s[ch], val)
                v.tensor_tensor(out=xslot(tt), in0=xslot(tt), in1=at3[:, tt, :],
                                op=ALU.add).then_inc(s["dve"], 1)
            pl.step("vector", v_x1)
            E_x1.append(pl.inc("dve"))

            sw = f"swl{tt % 2}"

            def g_pk(g, s, tt=tt, sw=sw):
                g.wait_ge(s["dve"], E_x1[tt])
                g.dma_start(out=x1r[tt], in_=xslot(tt)).then_inc(s[sw], 16)
            pl.step("gpsimd", g_pk)
            E_pk.append((sw, pl.inc(sw, 16)))

            bbuf = bnbuf[:, 48 * (tt % 2):48 * (tt % 2) + 48]

            def v_st(v, s, tt=tt, bbuf=bbuf):
                for i in range(8):
                    ins = v.bn_stats(bbuf[:, 6 * i:6 * i + 6],
                                     xslot(tt)[:, 512 * i:512 * i + 512])
                ins.then_inc(s["dve"], 1)
            pl.step("vector", v_st)
            m_st = pl.inc("dve")
            E_stats_f.append(plan_fence("dve", m_st))

        def plan_e_mid(tt):
            par = tt % 2
            ba = bnagg[:, 2 * par:2 * par + 2]
            rs = rstd[:, par:par + 1]
            nm = negmr[:, par:par + 1]

            def v_aggr(v, s, tt=tt, ba=ba, par=par):
                v.wait_ge(s["gps"], E_stats_f[tt])
                v.bn_aggr(ba, bnbuf[:, 48 * par:48 * par + 48]).then_inc(s["dve"], 1)
            pl.step("vector", v_aggr)
            m_aggr = pl.inc("dve")

            def sc_rstd(sc, s, ba=ba, rs=rs, m_aggr=m_aggr):
                sc.wait_ge(s["dve"], m_aggr)
                sc.activation(out=rs, in_=ba[:, 1:2], func=AF.Ln, bias=epsv[:, 0:1])
                sc.activation(out=rs, in_=rs, func=AF.Exp, scale=-0.5
                              ).then_inc(s["act"], 1)
            pl.step("scalar", sc_rstd)
            m_rstd = pl.inc("act")

            def v_negmr(v, s, ba=ba, rs=rs, nm=nm, m_rstd=m_rstd):
                v.wait_ge(s["act"], m_rstd)
                v.tensor_scalar(out=nm, in0=ba[:, 0:1], scalar1=rs, op0=ALU.mult,
                                scalar2=-1.0, op1=ALU.mult).then_inc(s["dve"], 1)
            pl.step("vector", v_negmr)
            m_negmr = pl.inc("dve")

            h2tm_free = E_tp_done[tt - 1] if tt >= 1 else 0

            def sc_n3(sc, s, tt=tt, rs=rs, nm=nm, m_negmr=m_negmr,
                      h2tm_free=h2tm_free):
                sc.wait_ge(s["dve"], m_negmr)
                if h2tm_free:
                    sc.wait_ge(s["pe"], h2tm_free)
                sc.activation(out=h1tm[:, :], in_=xslot(tt), func=AF.Identity,
                              scale=rs, bias=nm).then_inc(s["act"], 1)
            pl.step("scalar", sc_n3)
            E_nm2[tt] = pl.inc("act")

            plan_transpose_tile(
                src_fn=(lambda: h1tm),
                dst_col_fn=(lambda grp, tt=tt: A_h3[:, grp * 4:grp * 4 + 4,
                                                    tt * 128:tt * 128 + 128]),
                wait_act=E_nm2[tt], tp_banks=(0, 1), evac_hist=tp_hist_E)
            E_tp_done[tt] = pl.val("pe")

        plan_e_x0(1)
        for tt in range(NT):
            plan_e_front(tt)
            if tt >= 1:
                plan_e_mid(tt - 1)
            if 2 <= tt + 1 < NT:
                plan_e_x0(tt + 1)
        plan_e_mid(NT - 1)
        H2_DONE = tp_hist_E[-1]

        cexp_sb = A_xb[:, 4096:5120]
        lowp_sb = A_xb[:, 5120:6144]
        ffstat = lambda tt: (
            PC[:, (tt // 2) * 512 + (tt % 2) * 192:(tt // 2) * 512 + (tt % 2) * 192 + 192]
            if tt < 6 else PD[:, (tt - 6) * 192:(tt - 6) * 192 + 192])
        g_lfm = A_xb[0:8, 0:1024]
        g_cfm = A_xb[0:8, 1024:2048]

        def t_gate(t, s):
            t.wait_ge(s["dve"], H2_DONE)
            t.wait_ge(s["gld"], FF_WLD)
            gps_t = [PD[0:8, :], PC[0:8, 1024:1536]]
            for half in range(2):
                for kc in range(NG):
                    ins = t.matmul(gps_t[half], gw_sb[:, 8 * kc:8 * kc + 8],
                                   A_h3[:, kc, half * 512:half * 512 + 512],
                                   start=(kc == 0), stop=(kc == NG - 1))
            ins.then_inc(s["pe"], 1)
            for half in range(2):
                for kc in range(NG):
                    ins = t.matmul(PS[4 + half][:, :], wdq[:, 128 * kc:128 * kc + 128],
                                   A_h3[:, kc, half * 512:half * 512 + 512],
                                   start=(kc == 0), stop=(kc == NG - 1))
            ins.then_inc(s["pe"], 1)
        pl.step("tensor", t_gate)
        M_GMM = pl.inc("pe")
        M_LOWMM = pl.inc("pe")

        def sc_glog(sc, s):
            sc.wait_ge(s["pe"], M_GMM)
            sc.activation(out=g_lfm[:, 0:512], in_=PD[0:8, :], func=AF.Identity)
            sc.activation(out=g_lfm[:, 512:1024], in_=PC[0:8, 1024:1536],
                          func=AF.Identity).then_inc(s["act"], 1)
        pl.step("scalar", sc_glog)
        M_GLOG = pl.inc("act")

        def t_gtp(t, s):
            t.wait_ge(s["act"], M_GLOG)
            for tt in range(NT):
                pb = PS[2 + tt // 4]
                ins = t.matmul(pb[0:128, 8 * (tt % 4):8 * (tt % 4) + 8],
                               g_lfm[0:8, 128 * tt:128 * tt + 128], ident[0:8, 0:8],
                               start=(tt % 4 == 0), stop=True,
                               skip_group_check=(tt % 4 != 0))
            ins.then_inc(s["pe"], 1)
        pl.step("tensor", t_gtp)
        M_GTP = pl.inc("pe")

        def sc_ltm(sc, s):
            sc.wait_ge(s["pe"], M_GTP)
            sc.activation(out=g_ltm[:, 0:32], in_=PS[2][0:128, 0:32], func=AF.Identity)
            sc.activation(out=g_ltm[:, 32:64], in_=PS[3][0:128, 0:32],
                          func=AF.Identity).then_inc(s["act"], 1)
        pl.step("scalar", sc_ltm)
        M_LTM = pl.inc("act")

        def v_top_m1(v, s):
            v.wait_ge(s["act"], M_LTM)
            for tt in range(NT):
                ins = v.tensor_reduce(out=g_m1[:, tt:tt + 1], in_=g_ltm[:, 8 * tt:8 * tt + 8],
                                      op=ALU.max, axis=AX.X)
            ins.then_inc(s["dve"], 1)
        pl.step("vector", v_top_m1)
        M_G1 = pl.inc("dve")
        F_G1 = plan_fence("dve", M_G1)

        def v_top_eq1(v, s):
            v.wait_ge(s["gps"], F_G1)
            for tt in range(NT):
                v.tensor_scalar(out=g_eq1[:, 8 * tt:8 * tt + 8], in0=g_ltm[:, 8 * tt:8 * tt + 8],
                                scalar1=g_m1[:, tt:tt + 1], op0=ALU.is_ge, scalar2=None)
            for tt in range(NT):
                ins = v.scalar_tensor_tensor(out=g_msk[:, 8 * tt:8 * tt + 8],
                                             in0=g_eq1[:, 8 * tt:8 * tt + 8], scalar=-1e30,
                                             op0=ALU.mult, op1=ALU.add,
                                             in1=g_ltm[:, 8 * tt:8 * tt + 8])
            ins.then_inc(s["dve"], 1)
        pl.step("vector", v_top_eq1)
        M_G2 = pl.inc("dve")
        F_G2 = plan_fence("dve", M_G2)

        def v_top_m2(v, s):
            v.wait_ge(s["gps"], F_G2)
            for tt in range(NT):
                ins = v.tensor_reduce(out=g_m2[:, tt:tt + 1], in_=g_msk[:, 8 * tt:8 * tt + 8],
                                      op=ALU.max, axis=AX.X)
            ins.then_inc(s["dve"], 1)
        pl.step("vector", v_top_m2)
        M_G3 = pl.inc("dve")
        F_G3 = plan_fence("dve", M_G3)

        def v_top_eq2(v, s):
            v.wait_ge(s["gps"], F_G3)
            for tt in range(NT):
                ins = v.tensor_scalar(out=g_eq2[:, 8 * tt:8 * tt + 8],
                                      in0=g_msk[:, 8 * tt:8 * tt + 8],
                                      scalar1=g_m2[:, tt:tt + 1], op0=ALU.is_ge, scalar2=None)
            ins.then_inc(s["dve"], 1)
        pl.step("vector", v_top_eq2)
        M_TOPA = pl.inc("dve")

        def sc_gexp(sc, s):
            sc.wait_ge(s["dve"], M_TOPA)
            for tt in range(NT):
                ins = sc.activation(out=g_exp[:, tt:tt + 1], in_=g_m1[:, tt:tt + 1],
                                    func=AF.Exp, scale=-1.0, bias=g_m2[:, tt:tt + 1])
            ins.then_inc(s["act"], 1)
        pl.step("scalar", sc_gexp)
        M_GEXP = pl.inc("act")

        def v_top_b1(v, s):
            v.wait_ge(s["act"], M_GEXP)
            v.tensor_scalar(out=g_w2[:, :], in0=g_exp[:, :],
                            scalar1=1.0, op0=ALU.add, scalar2=None).then_inc(s["dve"], 1)
        pl.step("vector", v_top_b1)
        M_B1 = pl.inc("dve")
        F_B1 = plan_fence("dve", M_B1)

        def v_top_b2(v, s):
            v.wait_ge(s["gps"], F_B1)
            v.reciprocal(g_w1[:, :], g_w2[:, :]).then_inc(s["dve"], 1)
        pl.step("vector", v_top_b2)
        M_B2 = pl.inc("dve")
        F_B2 = plan_fence("dve", M_B2)

        def v_top_b3(v, s):
            v.wait_ge(s["gps"], F_B2)
            v.tensor_scalar(out=g_w2[:, :], in0=g_w1[:, :], scalar1=-1.0, op0=ALU.mult,
                            scalar2=1.0, op1=ALU.add).then_inc(s["dve"], 1)
        pl.step("vector", v_top_b3)
        M_TOPB = pl.inc("dve")

        def g_fence_w(g, s):
            g.wait_ge(s["dve"], M_TOPB)
            g.engine_nop().then_inc(s["gps"], 1)
        pl.step("gpsimd", g_fence_w)
        M_WF = pl.inc("gps")

        def v_top_c(v, s):
            v.wait_ge(s["gps"], M_WF)
            for tt in range(NT):
                v.tensor_scalar(out=g_ctm[:, 8 * tt:8 * tt + 8],
                                in0=g_eq1[:, 8 * tt:8 * tt + 8],
                                scalar1=g_w1[:, tt:tt + 1], op0=ALU.mult, scalar2=None)
            for tt in range(NT):
                v.scalar_tensor_tensor(out=g_ctm[:, 8 * tt:8 * tt + 8],
                                       in0=g_eq2[:, 8 * tt:8 * tt + 8],
                                       scalar=g_w2[:, tt:tt + 1], op0=ALU.mult,
                                       op1=ALU.add, in1=g_ctm[:, 8 * tt:8 * tt + 8])
            v.tensor_copy(g_ctmb[:, :], g_ctm[:, :]).then_inc(s["dve"], 1)
        pl.step("vector", v_top_c)
        M_TOP2 = pl.inc("dve")

        def t_ctp(t, s):
            t.wait_ge(s["dve"], M_TOP2)
            for tt in range(NT):
                pb = PS[2 + tt // 4]
                ins = t.matmul(pb[0:8, 128 * (tt % 4):128 * (tt % 4) + 128],
                               g_ctmb[:, 8 * tt:8 * tt + 8], ident[:, :],
                               start=(tt % 4 == 0), stop=True,
                               skip_group_check=(tt % 4 != 0))
            ins.then_inc(s["pe"], 1)
        pl.step("tensor", t_ctp)
        M_CTP = pl.inc("pe")

        def sc_cfm(sc, s):
            sc.wait_ge(s["pe"], M_CTP)
            sc.activation(out=g_cfm[:, 0:512], in_=PS[2][0:8, :], func=AF.Identity)
            sc.activation(out=g_cfm[:, 512:1024], in_=PS[3][0:8, :],
                          func=AF.Identity).then_inc(s["act"], 1)
        pl.step("scalar", sc_cfm)
        M_CFM = pl.inc("act")

        def t_cexp(t, s):
            t.wait_ge(s["act"], M_CFM)
            for half in range(2):
                ins = t.matmul(PS[2 + half][:, :], e816[:, :],
                               g_cfm[:, half * 512:half * 512 + 512], start=True, stop=True)
            ins.then_inc(s["pe"], 1)
        pl.step("tensor", t_cexp)
        M_CEXP = pl.inc("pe")

        def v_lowp(v, s):
            v.wait_ge(s["pe"], M_CEXP)
            v.tensor_copy(cexp_sb[:, 0:512], PS[2][:, :])
            v.tensor_copy(cexp_sb[:, 512:1024], PS[3][:, :])
            v.wait_ge(s["pe"], M_LOWMM)
            v.tensor_tensor(out=lowp_sb[:, 0:512], in0=PS[4][:, :], in1=cexp_sb[:, 0:512],
                            op=ALU.mult)
            v.tensor_tensor(out=lowp_sb[:, 512:1024], in0=PS[5][:, :],
                            in1=cexp_sb[:, 512:1024], op=ALU.mult).then_inc(s["dve"], 1)
        pl.step("vector", v_lowp)
        M_LOWP = pl.inc("dve")

        ff_bev = []
        ff_tp = []
        ff_tevac = []
        ff_base = []
        G_x1ld_early = {}
        for oc in range(32):
            if oc == 12:
                for tt_ in range(2):
                    gc_ = "gld" if tt_ % 2 == 0 else "gx1"

                    def g_x1e(g, s, tt_=tt_, gc_=gc_, pk_done=E_pk[tt_]):
                        g.wait_ge(s[pk_done[0]], pk_done[1])
                        g.wait_ge(s["act"], E_nm2[6 + tt_])
                        g.dma_start(out=xslot(tt_), in_=x1r[tt_]
                                    ).then_inc(s[gc_], 16)
                    pl.step("gpsimd", g_x1e)
                    G_x1ld_early[tt_] = (gc_, pl.inc(gc_, 16))
            if oc >= 2:
                bstr_free = ff_base[oc - 2]
                lsn = f"swl{oc % 2}"

                def g_bw(g, s, oc=oc, bstr_free=bstr_free, lsn=lsn):
                    if bstr_free:
                        g.wait_ge(s["pe"], bstr_free)
                    g.dma_start(
                        out=strips[:, 4096 * (oc % 2):4096 * (oc % 2) + 4096].rearrange(
                            "p (kc o) -> p kc o", o=128),
                        in_=ap["bwT"][:, 128 * oc:128 * oc + 128].rearrange(
                            "(kc p) o -> p kc o", p=128)).then_inc(s[lsn], 16)
                pl.step("gpsimd", g_bw)
                ff_bld.append((lsn, pl.inc(lsn, 16)))

            ao_free2 = ff_tp[2 * (oc - 2) + 1] if oc >= 2 else 0

            def t_base(t, s, oc=oc, ao_free2=ao_free2,
                       ev_prev=(ff_bev[oc - 1] if oc >= 1 else 0)):
                t.wait_ge(s[ff_bld[oc][0]], ff_bld[oc][1])
                if ev_prev:
                    t.wait_ge(s["act"], ev_prev)
                if oc == 0:
                    t.wait_ge(s["dve"], M_LOWP)
                sl = strips[:, 4096 * (oc % 2):4096 * (oc % 2) + 4096]
                for half in range(2):
                    for kc in range(NG):
                        t.matmul(PA[:, half * 512:half * 512 + 512],
                                 sl[:, 128 * kc:128 * kc + 128],
                                 A_h3[:, kc, half * 512:half * 512 + 512],
                                 start=(kc == 0), stop=False)
                    ins = t.matmul(PA[:, half * 512:half * 512 + 512],
                                   wdq[:, 4096 + 128 * oc:4096 + 128 * oc + 128],
                                   lowp_sb[:, half * 512:half * 512 + 512],
                                   start=False, stop=True)
                ins.then_inc(s["pe"], 1)
            pl.step("tensor", t_base)
            m_base = pl.inc("pe")
            ff_base.append(m_base)

            def sc_bev(sc, s, oc=oc, m_base=m_base,
                       ao_free=(ff_tp[2 * (oc - 2) + 1] if oc >= 2 else 0)):
                sc.wait_ge(s["pe"], m_base)
                if ao_free:
                    sc.wait_ge(s["pe"], ao_free)
                sc.activation(out=aofm(oc)[:, :], in_=PA[:, :], func=AF.Identity,
                              bias=bvec_sb[:, oc:oc + 1]).then_inc(s["act"], 1)
            pl.step("scalar", sc_bev)
            m_bev = pl.inc("act")
            ff_bev.append(m_bev)

            def plan_f_tp(oc2):
                m_bev2 = ff_bev[oc2]
                for grp in range(2):
                    pb = PS[2 + grp]
                    prev = ff_tevac[-2] if len(ff_tevac) >= 2 else 0

                    def t_tp(t, s, oc2=oc2, grp=grp, pb=pb, prev=prev,
                             m_bev2=m_bev2):
                        if grp == 0:
                            t.wait_ge(s["act"], m_bev2)
                        if prev:
                            t.wait_ge(s["dve"], prev)
                        for q in range(4):
                            ts_ = grp * 4 + q
                            ins = t.matmul(pb[:, 128 * q:128 * q + 128],
                                           aofm(oc2)[:, 128 * ts_:128 * ts_ + 128],
                                           ident[:, :], start=(q == 0), stop=True,
                                           skip_group_check=(q != 0))
                        ins.then_inc(s["pe"], 1)
                    pl.step("tensor", t_tp)
                    m_tp = pl.inc("pe")
                    ff_tp.append(m_tp)

                    def v_ev(v, s, oc2=oc2, grp=grp, pb=pb, m_tp=m_tp):
                        v.wait_ge(s["pe"], m_tp)
                        v.tensor_copy(at3[:, grp * 4:grp * 4 + 4,
                                          oc2 * 128:oc2 * 128 + 128],
                                      pb[:, :].rearrange("p (a b) -> p a b", a=4))
                        for q4 in range(4):
                            tt = grp * 4 + q4
                            ins = v.bn_stats(
                                ffstat(tt)[:, oc2 * 6:oc2 * 6 + 6],
                                at3[:, tt, oc2 * 128:oc2 * 128 + 128])
                        ins.then_inc(s["dve"], 1)
                    pl.step("vector", v_ev)
                    ff_tevac.append(pl.inc("dve"))

            if oc >= 1:
                plan_f_tp(oc - 1)
        plan_f_tp(31)
        F_DONE = ff_tevac[-1]

        G_norms, G_MAG = plan_ln_batch(F_DONE, region_fn=ffstat)
        G_x1ld = []
        G_out = []
        for tt in range(NT):
            gc = "gld" if tt % 2 == 0 else "gx1"
            lq = f"ldq{tt % 2}"
            lk = f"ldk{tt % 2}"

            if tt < 2:
                G_x1ld.append(G_x1ld_early[tt])
            else:
                def g_x1(g, s, tt=tt, gc=gc, pk_done=E_pk[tt]):
                    g.wait_ge(s[pk_done[0]], pk_done[1])
                    for ch, val in G_out[tt - 2]:
                        g.wait_ge(s[ch], val)
                    g.dma_start(out=xslot(tt), in_=x1r[tt]).then_inc(s[gc], 16)
                pl.step("gpsimd", g_x1)
                G_x1ld.append((gc, pl.inc(gc, 16)))

            m_nm = G_norms[tt]

            def v_add(v, s, tt=tt, m_nm=m_nm):
                v.wait_ge(s["act"], m_nm)
                v.wait_ge(s[G_x1ld[tt][0]], G_x1ld[tt][1])
                v.tensor_tensor(out=xslot(tt), in0=xslot(tt), in1=at3[:, tt, :],
                                op=ALU.add).then_inc(s["dve"], 1)
            pl.step("vector", v_add)
            m_add = pl.inc("dve")

            def sy_out(sy_, s, tt=tt, m_add=m_add):
                sy_.wait_ge(s["dve"], m_add)
                for j, ch in enumerate(("ldq0", "ldq1", "ldk0", "ldk1")):
                    sy_.dma_start(out=outr[tt][:, 1024 * j:1024 * j + 1024],
                                  in_=xslot(tt)[:, 1024 * j:1024 * j + 1024]
                                  ).then_inc(s[ch], 16)
            pl.step("sync", sy_out)
            G_out.append(tuple(
                (ch, pl.inc(ch, 16)) for ch in ("ldq0", "ldq1", "ldk0", "ldk1")))

        def sy_dbg(sy_, s):
            if "dbg_h1" in dbg:
                sy_.wait_ge(s["dve"], H1_DONE)
                sy_.dma_start(out=dbg["dbg_h1"][:, :], in_=A_h[:, 0:4096]
                              ).then_inc(s["dbs"], 16)
                pl.counts["dbs"] += 16
            if "dbg_contrib" in dbg:
                sy_.wait_ge(s["ldp0"], K_CONTRIB[0])
                sy_.wait_ge(s["ldp1"], K_CONTRIB[1])
                sy_.wait_ge(s["ldv0"], V_CONTRIB[0])
                sy_.wait_ge(s["ldv1"], V_CONTRIB[1])
                sy_.dma_start(out=dbg["dbg_contrib"][0:512, :], in_=cc_ap["ka_in"][:, :]
                              ).then_inc(s["dbs"], 16)
                sy_.dma_start(out=dbg["dbg_contrib"][512:1024, :], in_=cc_ap["kb_in"][:, :]
                              ).then_inc(s["dbs"], 16)
                sy_.dma_start(out=dbg["dbg_contrib"][1024:1536, :], in_=cc_ap["va_in"][:, :]
                              ).then_inc(s["dbs"], 16)
                sy_.dma_start(out=dbg["dbg_contrib"][1536:2048, :], in_=cc_ap["vb_in"][:, :]
                              ).then_inc(s["dbs"], 16)
                pl.counts["dbs"] += 64
            if "dbg_qpark" in dbg:
                sy_.wait_ge(s["ldp0"], Q_PARKED[0])
                sy_.wait_ge(s["ldp1"], Q_PARKED[1])
                sy_.dma_start(out=dbg["dbg_qpark"][:, :], in_=qpark[0:512, :]
                              ).then_inc(s["dbs"], 16)
                pl.counts["dbs"] += 16
            if "dbg_ctx" in dbg:
                sy_.wait_ge(s["dve"], CTX_DONE)
                sy_.dma_start(out=dbg["dbg_ctx"][:, :], in_=A_h[:, 0:4096]
                              ).then_inc(s["dbs"], 16)
                pl.counts["dbs"] += 16
            if "dbg_ao" in dbg:
                sy_.wait_ge(s["dve"], AO_DONE)
                sy_.dma_start(out=dbg["dbg_ao"][:, :], in_=A_big[:, 0:8192]
                              ).then_inc(s["dbs"], 16)
                pl.counts["dbs"] += 16
            if "dbg_x1" in dbg:
                sy_.wait_ge(s[E_pk[-1][0]], E_pk[-1][1])
                sy_.dma_start(out=dbg["dbg_x1"][:, :], in_=x1park[:, :]
                              ).then_inc(s["dbs"], 16)
                pl.counts["dbs"] += 16
            if "dbg_h2" in dbg:
                sy_.wait_ge(s["dve"], H2_DONE)
                sy_.dma_start(out=dbg["dbg_h2"][:, :], in_=A_h[:, 0:4096]
                              ).then_inc(s["dbs"], 16)
                pl.counts["dbs"] += 16
            if "dbg_gate" in dbg:
                sy_.wait_ge(s["dve"], M_TOP2)
                sy_.dma_start(out=dbg["dbg_gate"][:, :], in_=g_ctm[:, :]
                              ).then_inc(s["dbs"], 16)
                pl.counts["dbs"] += 16
            if "dbg_f" in dbg:
                sy_.wait_ge(s["dve"], F_DONE)
                sy_.dma_start(out=dbg["dbg_f"][:, :], in_=A_big[:, 0:8192]
                              ).then_inc(s["dbs"], 16)
                pl.counts["dbs"] += 16
        pl.step("sync", sy_dbg)

        def sy_final(sy_, s, a=None):
            for sem in ("ldx0", "ldx1", "ldp0", "ldp1",
                        "ldq0", "ldq1", "ldk0", "ldk1"):
                sy_.wait_ge(s[sem], pl.val(sem))
            if pl.val("dbs"):
                sy_.wait_ge(s["dbs"], pl.val("dbs"))
        pl.step("sync", sy_final)

        @block.sync
        def _(E):
            for f in pl.q["sync"]:
                f(E, semh)

        @block.tensor
        def _(E):
            for f in pl.q["tensor"]:
                f(E, semh)

        @block.scalar
        def _(E):
            for f in pl.q["scalar"]:
                f(E, semh)

        @block.vector
        def _(E):
            for f in pl.q["vector"]:
                f(E, semh)

        @block.gpsimd
        def _(E):
            for f in pl.q["gpsimd"]:
                f(E, semh)

    return nc


def reassemble(results):
    full = np.empty((NC * TPC, D), dtype=np.float32)
    fr = full.reshape(NC * TPC // 128, 128, D)
    for c in range(NC):
        fr[np.arange(NT) * 8 + c] = results[c]["out"].reshape(NT, 128, D)
    return full.reshape(1, NC * TPC, D)


_NC_CACHE = {}


def kernel(**inputs):
    per_core = host_prepare(inputs)
    if "nc" not in _NC_CACHE:
        _NC_CACHE["nc"] = build_nc(debug=())
    nc = _NC_CACHE["nc"]
    res = run_bass_kernel_spmd(nc, per_core, core_ids=list(range(NC)))
    return reassemble(res.results)
